# version 06
# baseline (speedup 1.0000x reference)
import numpy as np
import ml_dtypes
from contextlib import ExitStack
import concourse.bass as bass
import concourse.mybir as mybir
from concourse.bass_utils import run_bass_kernel_spmd

F32 = mybir.dt.float32
BF16 = mybir.dt.bfloat16
I32 = mybir.dt.int32
AF = mybir.ActivationFunctionType
ALU = mybir.AluOpType
AX = mybir.AxisListType

D = 2048
NHEAD = 16
NKV = 4
HD = 128
KVD = 512
TLOC = 2048
NCORES = 8
RMS_EPS = float(np.finfo(np.float32).eps)
LN_EPS = 1e-5
V_EPS2 = 1e-24
ISCALE = float(HD) ** -0.5
GC1 = 0.7978845608028654 * 0.044715
GC0 = 0.7978845608028654

DVE_OF16 = 10

_BF = ml_dtypes.bfloat16


class Ch:
    __slots__ = ("sem", "n")

    def __init__(self, sem):
        self.sem = sem
        self.n = 0

    def inc(self, k=1):
        self.n += k
        return self

    def tok(self):
        return (self.sem, self.n)


def build(dbg=()):
    nc = bass.Bass("TRN2", target_bir_lowering=False, debug=False)

    x_in = nc.dram_tensor("x_in", [TLOC, D], F32, kind="ExternalInput").ap()
    wsrc = {
        "wq": nc.dram_tensor("wq", [D, D], F32, kind="ExternalInput").ap(),
        "wk": nc.dram_tensor("wk", [KVD, D], F32, kind="ExternalInput").ap(),
        "wv": nc.dram_tensor("wv", [KVD, D], F32, kind="ExternalInput").ap(),
        "wo": nc.dram_tensor("wo", [D, D], F32, kind="ExternalInput").ap(),
    }
    gpk = nc.dram_tensor("gpk", [2048, 1024], I32, kind="ExternalInput").ap()
    gsct = nc.dram_tensor("gsct", [64, 2048], F32, kind="ExternalInput").ap()
    upk = nc.dram_tensor("upk", [2048, 1024], I32, kind="ExternalInput").ap()
    usct = nc.dram_tensor("usct", [64, 2048], F32, kind="ExternalInput").ap()
    dpk = nc.dram_tensor("dpk", [2048, 1024], I32, kind="ExternalInput").ap()
    dsct = nc.dram_tensor("dsct", [64, 2048], F32, kind="ExternalInput").ap()
    ident16 = nc.dram_tensor("ident16", [128, 128], BF16, kind="ExternalInput").ap()
    identf = nc.dram_tensor("identf", [128, 128], F32, kind="ExternalInput").ap()
    ones16 = nc.dram_tensor("ones16", [128, 128], BF16, kind="ExternalInput").ap()
    onesf = nc.dram_tensor("onesf", [128, 128], F32, kind="ExternalInput").ap()
    r4 = nc.dram_tensor("r4", [4, 128], F32, kind="ExternalInput").ap()
    out_d = nc.dram_tensor("out", [TLOC, D], F32, kind="ExternalOutput").ap()

    wT_d = {
        "wq": nc.dram_tensor("wqT_d", [D, D], BF16).ap(),
        "wk": nc.dram_tensor("wkT_d", [D, KVD], BF16).ap(),
        "wv": nc.dram_tensor("wvT_d", [D, KVD], BF16).ap(),
        "wo": nc.dram_tensor("woT_d", [D, D], BF16).ap(),
    }
    sct_d = [gsct, usct, dsct]
    kag_i = nc.dram_tensor("kag_i", [KVD, TLOC], BF16)
    kag_o = nc.dram_tensor("kag_o", [4 * KVD, TLOC], BF16)
    vag_i = nc.dram_tensor("vag_i", [TLOC, KVD], BF16)
    vag_o = nc.dram_tensor("vag_o", [4 * TLOC, KVD], BF16)
    gag_i = nc.dram_tensor("gag_i", [D, 2048], BF16)
    gag_o = nc.dram_tensor("gag_o", [NCORES * D, 2048], BF16)
    uag_i = nc.dram_tensor("uag_i", [D, 2048], BF16)
    uag_o = nc.dram_tensor("uag_o", [NCORES * D, 2048], BF16)
    dag_i = nc.dram_tensor("dag_i", [2048, D], BF16)
    dag_o = nc.dram_tensor("dag_o", [NCORES * 2048, D], BF16)
    x1_d = nc.dram_tensor("x1_d", [TLOC, D], F32).ap()

    dbg_d = {}
    for name, shape, dt in dbg:
        dbg_d[name] = nc.dram_tensor("dbg_" + name, shape, dt,
                                     kind="ExternalOutput").ap()

    EV = []

    def E(engine, fn):
        EV.append((engine, fn))

    def ew(engine, *toks):
        toks = [t for t in toks if t is not None and t[1] > 0]
        if toks:
            def f(e, toks=tuple(toks)):
                for s, v in toks:
                    e.wait_ge(s, v)
            E(engine, f)

    with ExitStack() as ctx:
        def sb(name, shape, dtype):
            return ctx.enter_context(nc.sbuf_tensor(name, shape, dtype))

        def psum(name, shape, dtype):
            return ctx.enter_context(nc.psum_tensor(name, shape, dtype))

        def sem(name):
            return Ch(ctx.enter_context(nc.semaphore(name)))

        arena_a = sb("arena_a", [128, 32768], BF16)
        arena_b = sb("arena_b", [128, 32768], BF16)
        arena_c = sb("arena_c", [128, 8192], F32)
        vnT = sb("vnT", [128, 8192], BF16)
        stag = sb("stag", [128, 4096], F32)
        wogrn = sb("wogrn", [128, 4096], BF16)
        kstg = sb("kstg", [128, 512], BF16)
        stats = sb("stats", [128, 16], F32)
        epsb = sb("epsb", [128, 4], F32)
        sid16 = sb("sid16", [128, 128], BF16)
        sidf = sb("sidf", [128, 128], F32)
        son16 = sb("son16", [128, 128], BF16)
        sonf = sb("sonf", [128, 128], F32)
        sr4 = sb("sr4", [4, 128], F32)
        sct_t = sb("sct_t", [4, 1024], F32)

        wrow16 = arena_b[:, 16384:18432]
        wstg16 = arena_b[:, 18432:20480]
        vts = wstg16
        vstg = kstg
        sq5 = wogrn.bitcast(F32)[:, 0:1024]
        st5 = wogrn.bitcast(F32)[:, 1024:2048]
        xn16_p1 = wogrn[:, 0:2048]
        xn16_p4 = vnT[:, 4096:6144]
        p4st = vnT.bitcast(F32)

        cF = arena_c
        t_e = cF[:, 0:2048].bitcast(I32)
        t_s2 = cF[:, 2048:4096].bitcast(I32)
        t_a2 = cF[:, 4096:6144]
        spk_v = cF[:, 6144:7168].bitcast(I32)
        vals_v = cF[:, 6144:7168].bitcast(BF16)
        svalT = cF[:, 7168:7424].bitcast(BF16)
        p1st = cF[:, 7424:7808]
        srstd = stag[:, 0:1024]
        sqrms = stag[:, 1024:1280].bitcast(BF16)
        acc_d = cF[:, 0:1024]
        acc_p = cF[:, 1024:2048]
        sp = [cF[:, 2048:2560].bitcast(BF16), cF[:, 2560:3072].bitcast(BF16),
              cF[:, 6656:7168].bitcast(BF16), cF[:, 7168:7680].bitcast(BF16)]
        prodb = cF[:, 3072:3584].bitcast(BF16)
        tmpc = cF[:, 3584:4608]
        recip = cF[:, 4608:5632]
        ytmp = cF[:, 5632:6656]
        outT = [cF[:, j * 512:(j + 1) * 512] for j in range(16)]
        accF = outT

        def xnT(jc):
            return arena_a[:, jc * 2048:(jc + 1) * 2048]
        xn1T = xnT

        def ktb(buf):
            return arena_a[:, buf * 8192:(buf + 1) * 8192]

        def vvb(buf):
            return arena_a[:, 16384 + buf * 8192:16384 + (buf + 1) * 8192]

        def qyb(h):
            return arena_b[:, h * 2048:(h + 1) * 2048]

        def interb(buf, jn):
            return arena_b[:, buf * 8192 + jn * 512:buf * 8192 + (jn + 1) * 512]

        def gslab(buf):
            return arena_b[:, 16384 + buf * 4096:16384 + (buf + 1) * 4096]

        def uslab(buf):
            return arena_b[:, 24576 + buf * 4096:24576 + (buf + 1) * 4096]

        def dslab(buf):
            return vnT[:, buf * 4096:(buf + 1) * 4096]

        pA = psum("pA", [128, 1024], F32)
        pB = psum("pB", [128, 1024], F32)
        pC = psum("pC", [128, 1024], F32)
        pD = psum("pD", [128, 1024], F32)
        pA16 = pA.bitcast(BF16)
        pB16 = pB.bitcast(BF16)
        pC16 = pC.bitcast(BF16)
        pD16 = pD.bitcast(BF16)
        accslot = [pA[:, 0:512], pA[:, 512:1024], pB[:, 0:512], pB[:, 512:1024]]

        ldq = sem("ldq")
        ldx = sem("ldx")
        ldp = sem("ldp")
        stq = sem("stq")
        c_pk = sem("c_pk")
        c_sct = [sem("c_sct0"), sem("c_sct1")]
        c_wrow = sem("c_wrow")
        c_awst = sem("c_awst")
        c_grn = [sem("c_grn0"), sem("c_grn1")]
        c_x = [sem("c_x0"), sem("c_x1")]
        c_kld = [sem("c_kld0"), sem("c_kld1")]
        c_vld = [sem("c_vld0"), sem("c_vld1")]
        c_kst = sem("c_kst")
        c_vst = sem("c_vst")
        c_gsl = [sem("c_gsl0"), sem("c_gsl1")]
        c_usl = [sem("c_usl0"), sem("c_usl1")]
        c_dsl = [sem("c_dsl0"), sem("c_dsl1")]
        c_x1 = sem("c_x1s")
        c_dbg = sem("c_dbg")
        pe = sem("pe")
        dv = sem("dv")
        ac = sem("ac")
        pl = sem("pl")
        cck = sem("cck")
        ccv = sem("ccv")
        ccw = [sem("ccg"), sem("ccu"), sem("ccd")]

        def dma(queue, dst, src, ch):
            E(queue, lambda e, d=dst, s=src, c=ch.inc(16): e.dma_start(
                out=d, in_=s).then_inc(c.sem, 16))
            return ch.tok()

        wT_r = {k: v.rearrange("(jc p) o -> p jc o", p=128)
                for k, v in wT_d.items()}

        def mm(acc_ap, lhsT_ap, rhs_ap, start, stop, mark=False):
            if mark:
                c = pe.inc()
                E("tensor", lambda e, a=acc_ap, l=lhsT_ap, r=rhs_ap, s1=start,
                  s2=stop, c=c: e.matmul(a, l, r, start=s1, stop=s2)
                  .then_inc(c.sem, 1))
            else:
                E("tensor", lambda e, a=acc_ap, l=lhsT_ap, r=rhs_ap, s1=start,
                  s2=stop: e.matmul(a, l, r, start=s1, stop=s2))

        def tp(dst_ap, src_ap, ident, mark=False):
            if mark:
                c = pe.inc()
                E("tensor", lambda e, d=dst_ap, s=src_ap, i=ident, c=c:
                  e.transpose(d, s, i).then_inc(c.sem, 1))
            else:
                E("tensor", lambda e, d=dst_ap, s=src_ap, i=ident:
                  e.transpose(d, s, i))

        deq_state = dict(spk_free=None, pd_free=None, sval_free=None,
                         wrow_free=None)
        shard_toks = [None, None, None]
        deq_queue = [(w, nt) for w in range(3) for nt in range(16)]
        pk_srcs = [gpk, upk, dpk]
        ag_is = [gag_i, uag_i, dag_i]

        def emit_decode_tile(job):
            wi, nt = job
            s = deq_state
            dmp_t = None
            ew("sync", s["spk_free"])
            pk_tok = dma("sync", spk_v[:],
                         pk_srcs[wi][nt * 128:(nt + 1) * 128, :], c_pk)
            if wi == 0 and nt == 0 and "dpk" in dbg_d:
                ew("scalar", pk_tok)
                pkd = dma("scalar", dbg_d["dpk"].bitcast(I32), spk_v[:], c_dbg)
                ew("vector", pkd)
            ew("vector", pk_tok, s["wrow_free"], consts_tok)
            def dvstep(fn):
                c = dv.inc()
                E("vector", lambda e, fn=fn, c=c: fn(e).then_inc(c.sem, 1))
                ew("vector", dv.tok())
            i2 = lambda ap: ap.rearrange("p (c two) -> p c two", two=2)
            dvstep(lambda e: e.tensor_scalar(
                out=i2(t_s2)[:, :, 0], in0=spk_v[:], scalar1=8, scalar2=28,
                op0=ALU.bitwise_and, op1=ALU.logical_shift_left))
            dvstep(lambda e: e.tensor_scalar(
                out=i2(t_s2)[:, :, 1], in0=spk_v[:], scalar1=0x80, scalar2=24,
                op0=ALU.bitwise_and, op1=ALU.logical_shift_left))
            dvstep(lambda e: e.tensor_scalar(
                out=i2(t_e)[:, :, 0], in0=spk_v[:], scalar1=7, scalar2=22,
                op0=ALU.bitwise_and, op1=ALU.logical_shift_left))
            dvstep(lambda e: e.tensor_scalar(
                out=i2(t_e)[:, :, 1], in0=spk_v[:], scalar1=0x70, scalar2=18,
                op0=ALU.bitwise_and, op1=ALU.logical_shift_left))
            dvstep(lambda e: e.tensor_scalar(
                out=t_e[:], in0=t_e[:], scalar1=0x40000000, scalar2=None,
                op0=ALU.add))
            dvstep(lambda e: e.tensor_scalar(
                out=t_a2[:], in0=t_e[:].bitcast(F32), scalar1=2.0,
                scalar2=-4.0, op0=ALU.mult, op1=ALU.add))
            dvstep(lambda e: e.tensor_tensor(
                out=t_a2[:], in0=t_e[:].bitcast(F32), in1=t_a2[:],
                op=ALU.min))
            dvstep(lambda e: e.tensor_scalar(
                out=t_s2[:], in0=t_s2[:], scalar1=0x3F800000, scalar2=None,
                op0=ALU.add))
            dvstep(lambda e: e.tensor_tensor(
                out=vals_v[:], in0=t_a2[:], in1=t_s2[:].bitcast(F32),
                op=ALU.mult))
            dec_tok = dv.tok()
            if wi == 0 and nt == 1 and "dvals1" in dbg_d:
                ew("scalar", dec_tok)
                s["spk_free"] = dma("scalar", dbg_d["dvals1"], vals_v[:], c_dbg)
            if wi == 0 and nt == 0 and "dte" in dbg_d:
                ew("scalar", dec_tok)
                ted = dma("scalar", dbg_d["dte"].bitcast(I32), t_e[:], c_dbg)
                tad = dma("scalar", dbg_d["dta"], t_a2[:], c_dbg)
                tsd = dma("scalar", dbg_d["dts"].bitcast(I32), t_s2[:], c_dbg)
                ew("tensor", tsd)
            if wi == 0 and nt == 0 and "dvals" in dbg_d:
                ew("scalar", dec_tok)
                s["spk_free"] = dma("scalar", dbg_d["dvals"], vals_v[:], c_dbg)
            for g in range(4):
                sb_ = g % 2
                ew("sync", s.get("sct_free%d" % sb_))
                sct_tok = dma(
                    "sync", sct_t[:, sb_ * 512:(sb_ + 1) * 512],
                    sct_d[wi].rearrange("(jc p) n -> p jc n", p=4)
                    [:, 4 * g:4 * g + 4, nt * 128:(nt + 1) * 128], c_sct[sb_])
                ew("tensor", dec_tok if g == 0 else None, sct_tok,
                   s["pd_free"], s["sval_free"])
                for k in range(4):
                    jc = g * 4 + k
                    tp(pD16[:, k * 128:(k + 1) * 128],
                       vals_v[:, jc * 128:(jc + 1) * 128], sid16[:])
                    mm(pD[:, 512 + k * 128:512 + (k + 1) * 128], sr4[:],
                       sct_t[:, sb_ * 512 + k * 128:sb_ * 512 + (k + 1) * 128],
                       True, True, mark=(k == 3))
                s["sct_free%d" % sb_] = (pe.sem, pe.n)
                grp_tok = pe.tok()
                ew("scalar", grp_tok, s["wrow_free"] if g == 0 else None)
                E("scalar", lambda e, c=ac.inc(): e.copy(
                    out=svalT[:], in_=pD16[:, 0:512]).then_inc(c.sem, 1))
                evt = ac.tok()
                ew("vector", evt)
                E("vector", lambda e, g=g, c=dv.inc(): e.tensor_tensor(
                    out=wrow16[:, g * 512:(g + 1) * 512], in0=svalT[:],
                    in1=pD[:, 512:1024], op=ALU.mult).then_inc(c.sem, 1))
                s["sval_free"] = dv.tok()
                s["pd_free"] = dv.tok()
            if not (wi == 0 and nt == 0 and "dvals" in dbg_d):
                s["spk_free"] = pe.tok()
            wrow_tok = dv.tok()
            if wi == 0 and nt == 1 and "dwrow1" in dbg_d:
                ew("scalar", wrow_tok)
                dmp_t = dma("scalar", dbg_d["dwrow1"], wrow16[:], c_dbg)
                s["spk_free"] = dmp_t
            if wi == 0 and nt == 0 and "dwrow" in dbg_d:
                ew("scalar", wrow_tok)
                dmp_t = dma("scalar", dbg_d["dwrow"], wrow16[:], c_dbg)
            ew("sync", wrow_tok,
               dmp_t if (wi == 0 and nt == 0 and "dwrow" in dbg_d) else None)
            s["wrow_free"] = dma(
                "sync",
                ag_is[wi].ap().rearrange("(jc p) n -> p jc n", p=128)
                [:, :, nt * 128:(nt + 1) * 128],
                wrow16[:].rearrange("p (jc r) -> p jc r", r=128), c_wrow)
            if nt == 15:
                shard_toks[wi] = c_wrow.tok()

        def pump(n=1):
            for _ in range(n):
                if deq_queue:
                    emit_decode_tile(deq_queue.pop(0))

        for dst, src in ((sid16, ident16), (sidf, identf), (son16, ones16),
                         (sonf, onesf), (sr4, r4)):
            dma("scalar", dst[:], src, ldx)
        consts_tok = ldx.tok()
        E("vector", lambda e: e.memset(epsb[:, 0:1], LN_EPS))
        E("vector", lambda e: e.memset(epsb[:, 1:2], RMS_EPS))
        E("vector", lambda e: e.memset(epsb[:, 2:3], V_EPS2))

        xtile = [stag[:, 0:2048], stag[:, 2048:4096]]
        xload_free = [None, None]

        ln_state = dict(xn_free=None)

        def ln_chain2(src_tile, ln_loaded, xb, tglob, pm, pq, sA, sB, sC,
                      tr_wait, ps_wait):
            dst3 = arena_a[:].rearrange("p (jc r) -> p jc r", r=2048)[
                :, :, tglob * 128:(tglob + 1) * 128]
            ew("vector", ln_loaded, consts_tok, ln_state["xn_free"])
            E("vector", lambda e, t=src_tile, xb=xb, c=dv.inc():
              e.tensor_copy(out=xb, in_=t).then_inc(c.sem, 1))
            cast_tok = dv.tok()
            ew("tensor", cast_tok, *tr_wait)
            for jc in range(16):
                tp(pA16[:, jc * 128:(jc + 1) * 128],
                   xb[:, jc * 128:(jc + 1) * 128], sid16[:], mark=(jc == 15))
            tr_tok = pe.tok()
            ew("scalar", tr_tok)
            E("scalar", lambda e, d=dst3, c=ac.inc(): e.copy(
                out=d, in_=pA16[:, 0:2048].rearrange(
                    "p (jc r) -> p jc r", r=128)).then_inc(c.sem, 1))
            evac_tok = ac.tok()
            ew("scalar", evac_tok)
            E("scalar", lambda e, d=dst3, xb=xb, c=ac.inc(): e.activation(
                out=xb, in_=d, func=AF.Square).then_inc(c.sem, 1))
            sqa_tok = ac.tok()
            ew("tensor", sqa_tok, *ps_wait)
            for jc in range(16):
                mm(pm, son16[:],
                   arena_a[:, jc * 2048 + tglob * 128:
                           jc * 2048 + tglob * 128 + 128],
                   jc == 0, jc == 15)
            for jc in range(16):
                mm(pq, son16[:], xb[:, jc * 128:(jc + 1) * 128],
                   jc == 0, jc == 15, mark=(jc == 15))
            smm_tok = pe.tok()
            ew("vector", smm_tok)
            E("vector", lambda e, pm=pm, sA=sA: e.tensor_scalar(
                out=sA, in0=pm, scalar1=1.0 / D, scalar2=None, op0=ALU.mult))
            E("vector", lambda e, sA=sA, sB=sB: e.tensor_tensor(
                out=sB, in0=sA, in1=sA, op=ALU.mult))
            E("vector", lambda e, pq=pq, sB=sB, c=dv.inc():
              e.scalar_tensor_tensor(
                  out=sB, in0=pq, scalar=1.0 / D, in1=sB,
                  op0=ALU.mult, op1=ALU.subtract).then_inc(c.sem, 1))
            var_tok = dv.tok()
            ew("scalar", var_tok)
            E("scalar", lambda e, sB=sB, sC=sC, c=ac.inc(): e.activation(
                out=sC, in_=sB, func=AF.Ln, bias=epsb[:, 0:1], scale=1.0)
              .then_inc(c.sem, 1))
            ln_tok = ac.tok()
            ew("scalar", ln_tok)
            E("scalar", lambda e, sC=sC, c=ac.inc(): e.activation(
                out=sC, in_=sC, func=AF.Exp, scale=-0.5).then_inc(c.sem, 1))
            exp_tok = ac.tok()
            ew("vector", exp_tok)
            E("vector", lambda e, d=dst3, sA=sA: e.tensor_tensor(
                out=d, in0=d, in1=sA.unsqueeze(1).broadcast_to([128, 16, 128]),
                op=ALU.subtract))
            E("vector", lambda e, d=dst3, sC=sC, c=dv.inc(): e.tensor_tensor(
                out=d, in0=d, in1=sC.unsqueeze(1).broadcast_to([128, 16, 128]),
                op=ALU.mult).then_inc(c.sem, 1))
            done_tok = dv.tok()
            ln_state["xn_free"] = smm_tok
            return done_tok, evac_tok, smm_tok, var_tok

        pa_free = None
        ps_free = None
        for tt in range(16):
            ew("scalar", xload_free[tt % 2])
            x_tok = dma("scalar", xtile[tt % 2],
                        x_in[tt * 128:(tt + 1) * 128, :], c_x[tt % 2])
            xn_tok, ev_t, smm_t, st_t = ln_chain2(
                xtile[tt % 2], x_tok, xn16_p1, tt,
                pC[:, 0:128], pC[:, 512:640],
                p1st[:, 0:128], p1st[:, 128:256], p1st[:, 256:384],
                tr_wait=(pa_free,), ps_wait=(ps_free,))
            xload_free[tt % 2] = xn_tok
            pa_free = ev_t
            ps_free = st_t
        xnT_all = dv.tok()
        if "xn" in dbg_d:
            ew("scalar", xnT_all)
            dma("scalar", dbg_d["xn"], arena_a[:], c_dbg)

        aw_state = dict(pb_free=None, wst_free=None)
        wT_written = {}

        def aw_prep(wname, ntile, first=None):
            for ot in range(ntile):
                ew("gpsimd", first if ot == 0 else None,
                   aw_state["wst_free"])
                wl_tok = dma("gpsimd", wstg16[:],
                             wsrc[wname][ot * 128:(ot + 1) * 128, :], ldp)
                ew("tensor", wl_tok, aw_state["pb_free"])
                for jc in range(16):
                    tp(pB16[:, jc * 128:(jc + 1) * 128],
                       wstg16[:, jc * 128:(jc + 1) * 128], sid16[:],
                       mark=(jc == 15))
                tr_tok = pe.tok()
                ew("vector", tr_tok)
                E("vector", lambda e, c=dv.inc(): e.tensor_copy(
                    out=wstg16[:], in_=pB16[:, 0:2048]).then_inc(c.sem, 1))
                ev_tok = dv.tok()
                aw_state["pb_free"] = ev_tok
                ew("sync", ev_tok)
                aw_state["wst_free"] = dma(
                    "sync", wT_r[wname][:, :, ot * 128:(ot + 1) * 128],
                    wstg16[:].rearrange("p (jc r) -> p jc r", r=128), c_awst)
                pump(1)
            wT_written[wname] = c_awst.tok()

        aw_prep("wk", 4)
        aw_evac_all = dv.tok()

        p2 = dict(slot=0, acc_free=[None] * 4, ms_free=ps_free)
        vts_done = [None] * 4
        vts_state = dict(free=None)
        kst_free = [None]
        vst_free = [None]
        pcV_free = [None]
        qproj_done = [None] * NHEAD
        grn_free = [None, None]

        def proj_tile(stat_fn, tq, kind, dest_jo, extra=()):
            sl = p2["slot"] % 4
            p2["slot"] += 1
            acc = accslot[sl]
            ew("tensor", *extra, p2["acc_free"][sl])
            for jc in range(16):
                mm(acc, stat_fn(jc), xnT(jc)[:, tq * 512:(tq + 1) * 512],
                   jc == 0, jc == 15, mark=(jc == 15))
            mm_tok = pe.tok()
            ew("scalar", mm_tok)
            E("scalar", lambda e, acc=acc, c=ac.inc(): e.activation(
                out=sqrms[:], in_=acc, func=AF.Square).then_inc(c.sem, 1))
            sq_tok = ac.tok()
            ew("tensor", sq_tok, p2["ms_free"])
            mm(pC[:, 0:512], son16[:], sqrms[:], True, True, mark=True)
            ms_tok = pe.tok()
            ew("scalar", ms_tok)
            if kind in ("q", "k"):
                E("scalar", lambda e: e.activation(
                    out=srstd[:, 0:512], in_=pC[:, 0:512], func=AF.Ln,
                    bias=epsb[:, 1:2], scale=1.0 / 128))
            else:
                E("scalar", lambda e: e.activation(
                    out=srstd[:, 0:512], in_=pC[:, 0:512], func=AF.Ln,
                    bias=epsb[:, 2:3], scale=1.0))
            E("scalar", lambda e, c=ac.inc(): e.activation(
                out=srstd[:, 512:1024], in_=srstd[:, 0:512], func=AF.Exp,
                scale=-0.5).then_inc(c.sem, 1))
            rstd_tok = ac.tok()
            p2["ms_free"] = rstd_tok
            if kind == "q":
                ew("vector", rstd_tok,
                   shard_toks[2] if (dest_jo == 8 and tq == 0
                                     and shard_toks[2]) else None)
                E("vector", lambda e, jo=dest_jo, tq=tq, acc=acc, c=dv.inc():
                  e.tensor_tensor(
                      out=qyb(jo)[:, tq * 512:(tq + 1) * 512], in0=acc,
                      in1=srstd[:, 512:1024], op=ALU.mult).then_inc(c.sem, 1))
                p2["acc_free"][sl] = dv.tok()
                qproj_done[dest_jo] = dv.tok()
            elif kind == "k":
                ew("vector", rstd_tok, kst_free[0])
                E("vector", lambda e, acc=acc, c=dv.inc(): e.tensor_tensor(
                    out=kstg[:], in0=acc, in1=srstd[:, 512:1024],
                    op=ALU.mult).then_inc(c.sem, 1))
                k_tok = dv.tok()
                p2["acc_free"][sl] = k_tok
                ew("scalar", k_tok)
                kst_free[0] = dma(
                    "scalar",
                    kag_i.ap()[dest_jo * 128:(dest_jo + 1) * 128,
                               tq * 512:(tq + 1) * 512], kstg[:], c_kst)
            else:
                ew("vector", rstd_tok, vts_state["free"],
                   wT_written["wv"] if vts_state["free"] is None else None)
                E("vector", lambda e, jo=dest_jo, tq=tq, acc=acc:
                  e.tensor_tensor(
                      out=vnT[:, jo * 2048 + tq * 512:
                              jo * 2048 + (tq + 1) * 512],
                      in0=acc, in1=srstd[:, 512:1024], op=ALU.mult))
                E("vector", lambda e, jo=dest_jo, acc=acc, c=dv.inc():
                  e.tensor_copy(
                      out=vts[:, jo * 512:(jo + 1) * 512], in_=acc)
                  .then_inc(c.sem, 1))
                p2["acc_free"][sl] = dv.tok()
                vts_done[dest_jo] = dv.tok()

        def load_grn(wname, jo, extra=()):
            gb = jo % 2
            ew("scalar", *extra, grn_free[gb])
            return gb, dma(
                "scalar",
                wogrn[:, gb * 2048:(gb + 1) * 2048].rearrange(
                    "p (jc r) -> p jc r", r=128),
                wT_r[wname][:, :, jo * 128:(jo + 1) * 128], c_grn[gb])

        for jo in range(4):
            gb, g_tok = load_grn(
                "wk", jo,
                extra=(wT_written["wk"], xnT_all) if jo == 0 else ())
            for tq in range(4):
                proj_tile(
                    lambda jc, gb=gb: wogrn[:, gb * 2048 + jc * 128:
                                            gb * 2048 + (jc + 1) * 128],
                    tq, "k", jo,
                    extra=((g_tok, xnT_all, aw_evac_all)
                           if (jo == 0 and tq == 0) else
                           (g_tok,) if tq == 0 else ()))
                pump(2)
            grn_free[gb] = pe.tok()

        aw_prep("wv", 4)
        for tq in range(4):
            for jo in range(4):
                gb, g_tok = load_grn(
                    "wv", jo,
                    extra=(wT_written["wv"],) if (tq == 0 and jo == 0) else ())
                proj_tile(
                    lambda jc, gb=gb: wogrn[:, gb * 2048 + jc * 128:
                                            gb * 2048 + (jc + 1) * 128],
                    tq, "v", jo, extra=(g_tok,))
                grn_free[gb] = pe.tok()
            for tc in range(4):
                tglob = tq * 4 + tc
                ew("tensor", vts_done[3], pcV_free[0])
                for jo in range(4):
                    tp(pC16[:, 1024 + jo * 128:1024 + (jo + 1) * 128],
                       vts[:, jo * 512 + tc * 128:jo * 512 + (tc + 1) * 128],
                       sid16[:], mark=(jo == 3))
                vt_tok = pe.tok()
                ew("scalar", vt_tok, vst_free[0],
                   kst_free[0] if tglob == 0 else None)
                E("scalar", lambda e, c=ac.inc(): e.copy(
                    out=vstg[:], in_=pC16[:, 1024:1536]).then_inc(c.sem, 1))
                ev_tok = ac.tok()
                pcV_free[0] = ev_tok
                ew("scalar", ev_tok)
                vst_free[0] = dma(
                    "scalar", vag_i.ap()[tglob * 128:(tglob + 1) * 128, :],
                    vstg[:], c_vst)
            vts_state["free"] = pe.tok()
            pump(2)
        kag_done = kst_free[0]
        vag_done = vst_free[0]
        v_mm_done = pe.tok()

        GROUPS = [[0, 1, 2, 3], [4, 5, 6, 7]]
        ew("gpsimd", kag_done)
        E("gpsimd", lambda e, c=cck.inc(): e.collective_compute(
            "AllGather", ALU.bypass, replica_groups=GROUPS,
            ins=[kag_i.ap().opt()], outs=[kag_o.ap().opt()]).then_inc(c.sem, 1))
        ew("gpsimd", vag_done)
        E("gpsimd", lambda e, c=ccv.inc(): e.collective_compute(
            "AllGather", ALU.bypass, replica_groups=GROUPS,
            ins=[vag_i.ap().opt()], outs=[vag_o.ap().opt()]).then_inc(c.sem, 1))

        aw_prep("wq", 16, first=pe.tok())
        aw_prep("wo", 16)
        for jo in range(16):
            gb, g_tok = load_grn(
                "wq", jo,
                extra=((wT_written["wq"], v_mm_done) if jo == 0 else ()))
            for tq in range(4):
                proj_tile(
                    lambda jc, gb=gb: wogrn[:, gb * 2048 + jc * 128:
                                            gb * 2048 + (jc + 1) * 128],
                    tq, "q", jo, extra=(g_tok,) if tq == 0 else ())
                pump(1)
            grn_free[gb] = pe.tok()
        qproj_all = dv.tok()
        p2_pe_done = pe.tok()
        pump(64)

        for wi in range(3):
            ew("gpsimd", shard_toks[wi])
            ag_o2 = [gag_o, uag_o, dag_o][wi]
            ag_i2 = ag_is[wi]
            E("gpsimd", lambda e, ag_i2=ag_i2, ag_o2=ag_o2, c=ccw[wi].inc():
              e.collective_compute(
                  "AllGather", ALU.bypass,
                  replica_groups=[list(range(NCORES))],
                  ins=[ag_i2.ap().opt()], outs=[ag_o2.ap().opt()])
              .then_inc(c.sem, 1))

        if "qy" in dbg_d:
            ew("scalar", qproj_all)
            dma("scalar", dbg_d["qy"], arena_b[:, 0:32768], c_dbg)
        if "vn" in dbg_d:
            ew("scalar", qproj_all)
            dma("scalar", dbg_d["vn"], vnT[:], c_dbg)
        if "gag" in dbg_d:
            ew("scalar", ccw[0].tok())
            dma("scalar", dbg_d["gag"], gag_o.ap(), stq)
        if "dag" in dbg_d:
            ew("scalar", ccw[2].tok())
            dma("scalar", dbg_d["dag"], dag_o.ap(), stq)
        if "kag" in dbg_d:
            ew("scalar", cck.tok())
            dma("scalar", dbg_d["kag"], kag_o.ap(), stq)
        if "vag" in dbg_d:
            ew("scalar", ccv.tok())
            dma("scalar", dbg_d["vag"], vag_o.ap(), stq)

        pS = pA
        kld = [None, None]
        vld = [None, None]
        kuse = [None, None]
        vuse = [None, None]
        pS_free = [None, None]
        sp_free = [None] * 4
        pB_drain = [None, None]
        exp_tok_q = [None] * 128
        pC_drain = [None, None]
        pD_drain = [None, None]
        y_done = [[None] * 4 for _ in range(NHEAD)]

        for g in range(NKV):
            kb = g % 2
            ew("scalar", cck.tok() if g == 0 else None,
               p2_pe_done if g == 0 else None, kuse[kb])
            for rb in range(4):
                kld[kb] = dma(
                    "scalar", ktb(kb)[:, rb * 2048:(rb + 1) * 2048],
                    kag_o.ap()[rb * KVD + g * 128:rb * KVD + (g + 1) * 128, :],
                    c_kld[kb])
            ew("scalar", ccv.tok() if g == 0 else None, vuse[kb])
            vld[kb] = dma(
                "scalar", vvb(kb)[:],
                vag_o.ap().rearrange("(j p) d -> p j d", p=128)
                [:, :, g * 128:(g + 1) * 128], c_vld[kb])
            for hp in range(2):
                hA = 4 * g + 2 * hp
                for tq in range(4):
                    qsl = slice(tq * 512, (tq + 1) * 512)
                    for j in range(65):
                        if j < 64:
                            for hi in range(2):
                                h = hA + hi
                                ew("tensor",
                                   kld[kb] if (hp == 0 and tq == 0 and j == 0
                                               and hi == 0) else None,
                                   qproj_done[h] if j == 0 else None,
                                   qproj_all if (g == 0 and hp == 0 and tq == 0
                                                 and j == 0 and hi == 0)
                                   else None,
                                   pS_free[hi] if hi == 0 else None)
                                mm(pS[:, hi * 512:(hi + 1) * 512],
                                   ktb(kb)[:, j * 128:(j + 1) * 128],
                                   qyb(h)[:, qsl], True, True, mark=(hi == 1))
                            sc_tok = pe.tok()
                            kuse[kb] = sc_tok
                            sps = j % 4
                            ew("scalar", sc_tok, sp_free[sps])
                            E("scalar", lambda e, sps=sps, c=ac.inc():
                              e.activation(out=sp[sps][:], in_=pS[:],
                                           func=AF.Exp, scale=ISCALE)
                              .then_inc(c.sem, 1))
                            pS_free[0] = ac.tok()
                            exp_tok_q[j] = ac.tok()
                        for hi in range(2):
                            if j > 0:
                                jj = j - 1
                                sps = jj % 4
                                spv = sp[sps][:, hi * 512:(hi + 1) * 512]
                                ew("tensor", exp_tok_q[jj] if hi == 0 else None,
                                   vld[kb] if (hp == 0 and tq == 0 and jj == 0
                                               and hi == 0) else None,
                                   pC_drain[hi] if jj == 0 else None,
                                   pB_drain[hi] if jj == 0 else None)
                                mm(pC[:, hi * 512:(hi + 1) * 512],
                                   vvb(kb)[:, jj * 128:(jj + 1) * 128],
                                   spv, jj == 0, jj == 63)
                                mm(pB[:, hi * 512:(hi + 1) * 512], son16[:],
                                   spv, jj == 0, jj == 63, mark=(hi == 1))
                                pv_tok = pe.tok()
                                if hi == 1:
                                    sp_free[sps] = pv_tok
                                vuse[kb] = pv_tok
                    for hi in range(2):
                        h = hA + hi
                        hs = slice(hi * 512, (hi + 1) * 512)
                        ew("scalar", pe.tok())
                        E("scalar", lambda e, hs=hs, c=ac.inc():
                          e.activation(out=recip[:, hs], in_=pB[:, hs],
                                       func=AF.Ln, scale=1.0)
                          .then_inc(c.sem, 1))
                        pB_drain[hi] = ac.tok()
                        ew("scalar", ac.tok())
                        E("scalar", lambda e, hs=hs, c=ac.inc():
                          e.activation(out=recip[:, hs], in_=recip[:, hs],
                                       func=AF.Exp, scale=-1.0)
                          .then_inc(c.sem, 1))
                        rc_tok = ac.tok()
                        ew("vector", rc_tok)
                        E("vector", lambda e, hs=hs, c=dv.inc():
                          e.tensor_tensor(
                              out=ytmp[:, hs], in0=pC[:, hs],
                              in1=recip[:, hs], op=ALU.mult)
                          .then_inc(c.sem, 1))
                        pC_drain[hi] = dv.tok()
                        E("vector", lambda e, g=g, hs=hs, qsl=qsl, c=dv.inc():
                          e.tensor_tensor(
                              out=prodb[:, hs], in0=ytmp[:, hs],
                              in1=vnT[:, g * 2048:(g + 1) * 2048][:, qsl],
                              op=ALU.mult).then_inc(c.sem, 1))
                        pr_tok = dv.tok()
                        ew("tensor", pr_tok, pD_drain[hi])
                        mm(pD[:, hs], son16[:], prodb[:, hs], True, True,
                           mark=True)
                        sr_tok = pe.tok()
                        ew("vector", sr_tok,
                           shard_toks[2] if (h == 8 and tq == 0) else None)
                        E("vector", lambda e, g=g, hs=hs, qsl=qsl, c=dv.inc():
                          e.tensor_tensor(
                              out=tmpc[:, hs], in0=pD[:, hs],
                              in1=vnT[:, g * 2048:(g + 1) * 2048][:, qsl],
                              op=ALU.mult).then_inc(c.sem, 1))
                        pD_drain[hi] = dv.tok()
                        E("vector", lambda e, h=h, hs=hs, qsl=qsl, c=dv.inc():
                          e.tensor_tensor(
                              out=qyb(h)[:, qsl], in0=ytmp[:, hs],
                              in1=tmpc[:, hs], op=ALU.subtract)
                          .then_inc(c.sem, 1))
                        y_done[h][tq] = dv.tok()
        p3_dve = dv.tok()
        p3_pool = pl.tok()
        p3_pe = pe.tok()
        if "y" in dbg_d:
            ew("scalar", y_done[NHEAD - 1][1])
            dma("scalar", dbg_d["y"], arena_b[:, 0:32768], c_dbg)

        ogrn_free = [None, None]
        oacc_free = [None, None, None, None]
        outT_free = [None] * 16
        x4_free = [None, None]
        x1st = [None]
        pc4_free = None
        pd4_free = None
        xn1_last = None

        for qb in range(4):
            jo_last = None
            for jo in range(16):
                gb = jo % 2
                ew("sync", wT_written["wo"] if (qb == 0 and jo == 0) else None,
                   qproj_all if (qb == 0 and jo == 0) else None,
                   ogrn_free[gb])
                g_tok = dma("sync",
                            wogrn[:, gb * 2048:(gb + 1) * 2048].rearrange(
                                "p (jc r) -> p jc r", r=128),
                            wT_r["wo"][:, :, jo * 128:(jo + 1) * 128],
                            c_grn[gb])
                sl = (qb * 16 + jo) % 4
                acc = accslot[sl]
                ew("tensor", g_tok, oacc_free[sl],
                   y_done[NHEAD - 1][qb] if jo == 0 else None)
                for jc in range(16):
                    mm(acc, wogrn[:, gb * 2048 + jc * 128:
                                  gb * 2048 + (jc + 1) * 128],
                       qyb(jc)[:, qb * 512:(qb + 1) * 512],
                       jc == 0, jc == 15, mark=(jc == 15))
                mm_tok = pe.tok()
                ogrn_free[gb] = mm_tok
                ew("scalar", mm_tok,
                   p3_dve if (qb == 0 and jo == 0) else None,
                   p3_pool if (qb == 0 and jo == 0) else None,
                   outT_free[jo])
                E("scalar", lambda e, jo=jo, acc=acc, c=ac.inc(): e.copy(
                    out=outT[jo], in_=acc).then_inc(c.sem, 1))
                oacc_free[sl] = ac.tok()
                jo_last = ac.tok()
            for tb in range(4):
                tglob = qb * 4 + tb
                ew("scalar", x4_free[tb % 2], x1st[0])
                xl_tok = dma("scalar", xtile[tb % 2],
                             x_in[tglob * 128:(tglob + 1) * 128, :],
                             c_x[tb % 2])
                ew("tensor", jo_last, pc4_free, pd4_free,
                   pC_drain[1] if (qb == 0 and tb == 0) else None,
                   pD_drain[1] if (qb == 0 and tb == 0) else None)
                for jo in range(16):
                    dst = pC if jo < 8 else pD
                    off = (jo % 8) * 128
                    tp(dst[:, off:off + 128],
                       outT[jo][:, tb * 128:(tb + 1) * 128], sidf[:],
                       mark=(jo == 15))
                tr_tok = pe.tok()
                if tb == 3:
                    for jo in range(16):
                        outT_free[jo] = tr_tok
                ew("vector", tr_tok, xl_tok, x1st[0])
                E("vector", lambda e, t=xtile[tb % 2]: e.tensor_tensor(
                    out=t[:, 0:1024], in0=t[:, 0:1024], in1=pC[:],
                    op=ALU.add))
                E("vector", lambda e, t=xtile[tb % 2], c=dv.inc():
                  e.tensor_tensor(
                      out=t[:, 1024:2048], in0=t[:, 1024:2048], in1=pD[:],
                      op=ALU.add).then_inc(c.sem, 1))
                x1_tok = dv.tok()
                pc4_free = x1_tok
                pd4_free = x1_tok
                ew("scalar", x1_tok)
                x1st[0] = dma("scalar",
                              x1_d[tglob * 128:(tglob + 1) * 128, :],
                              xtile[tb % 2], c_x1)
                xn_tok, ev_t, smm_t, st_t = ln_chain2(
                    xtile[tb % 2], x1_tok, xn16_p4, tglob,
                    pC[:, 0:128], pD[:, 0:128],
                    p4st[:, 0:128], p4st[:, 128:256], p4st[:, 256:384],
                    tr_wait=(p3_pe if tglob == 0 else None,),
                    ps_wait=(None,))
                x4_free[tb % 2] = xn_tok
                pc4_free = st_t
                pd4_free = st_t
                xn1_last = xn_tok
                oacc_free[0] = smm_t
                oacc_free[1] = smm_t
        xn1T_all = xn1_last
        x1_all = x1st[0]
        p4_pe_done = pe.tok()
        if "x1" in dbg_d:
            ew("scalar", x1_all)
            dma("scalar", dbg_d["x1"], x1_d, stq)

        gag_r = gag_o.ap().rearrange("(r jc p) n -> p r jc n", p=128, jc=16)
        uag_r = uag_o.ap().rearrange("(r jc p) n -> p r jc n", p=128, jc=16)
        dag_r = dag_o.ap().rearrange("(r jn p) h -> p r jn h", p=128, jn=16)
        gsl_free = [None, None]
        usl_free = [None, None]
        dsl_free = [None, None]
        int_free = [None, None]
        gu_free = [None, None]
        dw_free = [None, None]
        acc5_tok = [None] * 16
        scr_free = [None, None]
        x15_free = [None, None]
        out5_free = [None]
        fin_ps_free = None

        for tq4 in range(4):
            for r in range(NCORES):
                it = r % 2
                int_done = None
                for grp in range(8):
                    gb = grp % 2
                    first5 = tq4 == 0 and r == 0 and grp == 0
                    ew("sync",
                       ccw[0].tok() if first5 else None,
                       ccw[1].tok() if first5 else None,
                       p4_pe_done if first5 else None,
                       gsl_free[gb])
                    gt = dma("sync",
                             gslab(gb).rearrange("p (jc r_) -> p jc r_",
                                                 r_=256),
                             gag_r[:, r, :, grp * 256:(grp + 1) * 256],
                             c_gsl[gb])
                    ew("sync", usl_free[gb])
                    ut = dma("sync",
                             uslab(gb).rearrange("p (jc r_) -> p jc r_",
                                                 r_=256),
                             uag_r[:, r, :, grp * 256:(grp + 1) * 256],
                             c_usl[gb])
                    gps, ups = ((pA, pB), (pC, pD))[gb]
                    ew("tensor", gt, ut, gu_free[gb],
                       xn1T_all if first5 else None)
                    for jc in range(16):
                        for k in range(2):
                            mm(gps[:, k * 512:(k + 1) * 512],
                               gslab(gb)[:, jc * 256 + k * 128:
                                         jc * 256 + (k + 1) * 128],
                               xn1T(jc)[:, tq4 * 512:(tq4 + 1) * 512],
                               jc == 0, jc == 15)
                        for k in range(2):
                            mm(ups[:, k * 512:(k + 1) * 512],
                               uslab(gb)[:, jc * 256 + k * 128:
                                         jc * 256 + (k + 1) * 128],
                               xn1T(jc)[:, tq4 * 512:(tq4 + 1) * 512],
                               jc == 0, jc == 15,
                               mark=(jc == 15 and k == 1))
                    gu_tok = pe.tok()
                    gsl_free[gb] = gu_tok
                    usl_free[gb] = gu_tok
                    for k in range(2):
                        jn = grp * 2 + k
                        sc = k
                        A = sq5[:, sc * 512:(sc + 1) * 512]
                        B = st5[:, sc * 512:(sc + 1) * 512]
                        gp = gps[:, k * 512:(k + 1) * 512]
                        up = ups[:, k * 512:(k + 1) * 512]
                        ew("scalar", gu_tok, scr_free[sc])
                        E("scalar", lambda e, gp=gp, A=A, c=ac.inc():
                          e.activation(out=A, in_=gp, func=AF.Square)
                          .then_inc(c.sem, 1))
                        sq_tok = ac.tok()
                        ew("vector", sq_tok,
                           int_free[it] if jn == 0 else None)
                        E("vector", lambda e, A=A, B=B: e.tensor_scalar(
                            out=B, in0=A, scalar1=GC1, scalar2=GC0,
                            op0=ALU.mult, op1=ALU.add))
                        E("vector", lambda e, gp=gp, A=A, B=B, c=dv.inc():
                          e.tensor_tensor(out=A, in0=B, in1=gp, op=ALU.mult)
                          .then_inc(c.sem, 1))
                        arg_tok = dv.tok()
                        ew("scalar", arg_tok)
                        E("scalar", lambda e, A=A, B=B, c=ac.inc():
                          e.activation(out=B, in_=A, func=AF.Tanh)
                          .then_inc(c.sem, 1))
                        th_tok = ac.tok()
                        ew("vector", th_tok)
                        E("vector", lambda e, B=B: e.tensor_scalar(
                            out=B, in0=B, scalar1=0.5, scalar2=0.5,
                            op0=ALU.mult, op1=ALU.add))
                        E("vector", lambda e, gp=gp, A=A, B=B:
                          e.tensor_tensor(out=A, in0=B, in1=gp, op=ALU.mult))
                        E("vector", lambda e, up=up, A=A, it=it, jn=jn,
                          c=dv.inc(): e.tensor_tensor(
                              out=interb(it, jn), in0=A, in1=up, op=ALU.mult)
                          .then_inc(c.sem, 1))
                        scr_free[sc] = dv.tok()
                        if grp == 7 and k == 1:
                            int_done = dv.tok()
                        gu_free[gb] = dv.tok()
                for hg in range(8):
                    db = hg % 2
                    firstd = tq4 == 0 and r == 0 and hg == 0
                    ew("sync",
                       ccw[2].tok() if firstd else None,
                       p3_dve if firstd else None,
                       x1_all if firstd else None,
                       xn1T_all if firstd else None,
                       dsl_free[db])
                    dt = dma("sync",
                             dslab(db).rearrange("p (jn r_) -> p jn r_",
                                                 r_=256),
                             dag_r[:, r, :, hg * 256:(hg + 1) * 256],
                             c_dsl[db])
                    dps = (pA, pB)[db]
                    ew("tensor", dt, int_done if hg == 0 else None,
                       dw_free[db])
                    for jn in range(16):
                        for k in range(2):
                            mm(dps[:, k * 512:(k + 1) * 512],
                               dslab(db)[:, jn * 256 + k * 128:
                                         jn * 256 + (k + 1) * 128],
                               interb(it, jn), jn == 0, jn == 15,
                               mark=(jn == 15 and k == 1))
                    dn_tok = pe.tok()
                    dsl_free[db] = dn_tok
                    if hg == 7:
                        int_free[it] = dn_tok
                    for k in range(2):
                        h = hg * 2 + k
                        ew("vector", dn_tok if k == 0 else None,
                           acc5_tok[h])
                        if r == 0:
                            E("vector", lambda e, h=h, k=k, dps=dps,
                              c=dv.inc(): e.tensor_copy(
                                  out=accF[h],
                                  in_=dps[:, k * 512:(k + 1) * 512])
                              .then_inc(c.sem, 1))
                        else:
                            E("vector", lambda e, h=h, k=k, dps=dps,
                              c=dv.inc(): e.tensor_tensor(
                                  out=accF[h], in0=accF[h],
                                  in1=dps[:, k * 512:(k + 1) * 512],
                                  op=ALU.add).then_inc(c.sem, 1))
                        acc5_tok[h] = dv.tok()
                        dw_free[db] = dv.tok()
                gu_free[0] = dv.tok()
            for tb in range(4):
                tglob = tq4 * 4 + tb
                ew("scalar", x15_free[tb % 2])
                x1l = dma("scalar", xtile[tb % 2],
                          x1_d[tglob * 128:(tglob + 1) * 128, :],
                          c_x[tb % 2])
                ew("tensor", acc5_tok[15], fin_ps_free,
                   gu_free[1] if tb == 0 else None)
                for h in range(16):
                    dst = pC if h < 8 else pD
                    off = (h % 8) * 128
                    tp(dst[:, off:off + 128],
                       accF[h][:, tb * 128:(tb + 1) * 128], sidf[:],
                       mark=(h == 15))
                ftr_tok = pe.tok()
                ew("vector", ftr_tok, x1l, out5_free[0])
                E("vector", lambda e, t=xtile[tb % 2]: e.tensor_tensor(
                    out=t[:, 0:1024], in0=t[:, 0:1024], in1=pC[:],
                    op=ALU.add))
                E("vector", lambda e, t=xtile[tb % 2], c=dv.inc():
                  e.tensor_tensor(
                      out=t[:, 1024:2048], in0=t[:, 1024:2048], in1=pD[:],
                      op=ALU.add).then_inc(c.sem, 1))
                fo_tok = dv.tok()
                fin_ps_free = fo_tok
                ew("scalar", fo_tok)
                out5_free[0] = dma("scalar",
                                   out_d[tglob * 128:(tglob + 1) * 128, :],
                                   xtile[tb % 2], c_dbg)
                x15_free[tb % 2] = out5_free[0]
            if tq4 < 3:
                final_tr = pe.tok()
                for h in range(16):
                    acc5_tok[h] = final_tr
                gu_free[1] = dv.tok()

        final_store = stq.tok()
        E("sync", lambda e, t=final_store: e.wait_ge(t[0], t[1]))

        with nc.Block() as block:
            @block.sync
            def _(eng):
                for g_, f in EV:
                    if g_ == "sync":
                        f(eng)

            @block.scalar
            def _(eng):
                for g_, f in EV:
                    if g_ == "scalar":
                        f(eng)

            @block.vector
            def _(eng):
                for g_, f in EV:
                    if g_ == "vector":
                        f(eng)

            @block.tensor
            def _(eng):
                for g_, f in EV:
                    if g_ == "tensor":
                        f(eng)

            @block.gpsimd
            def _(eng):
                for g_, f in EV:
                    if g_ == "gpsimd":
                        f(eng)

    return nc


_NC_CACHE = {}


def _get_nc(dbg=()):
    key = tuple(x[0] for x in dbg)
    if key not in _NC_CACHE:
        _NC_CACHE[key] = build(dbg)
    return _NC_CACHE[key]


def _consts():
    r4v = np.zeros((4, 128), dtype=np.float32)
    for b_ in range(4):
        r4v[b_, b_ * 32:(b_ + 1) * 32] = 0.25
    return dict(ident16=np.eye(128, dtype=_BF),
                identf=np.eye(128, dtype=np.float32),
                ones16=np.ones((128, 128), dtype=_BF),
                onesf=np.ones((128, 128), dtype=np.float32),
                r4=r4v)


def make_in_maps(x, wq_, wk_, wv_, wo_, gp, gs, up, us, dp, ds):
    consts = _consts()
    in_maps = []
    for c in range(NCORES):
        b, qs = divmod(c, 4)
        rows = slice(c * 2048, (c + 1) * 2048)
        in_maps.append(dict(
            x_in=np.ascontiguousarray(x[b, qs * 2048:(qs + 1) * 2048, :]),
            wq=wq_, wk=wk_, wv=wv_, wo=wo_,
            gpk=np.ascontiguousarray(gp[rows, :]),
            gsct=np.ascontiguousarray(gs[rows, :].T),
            upk=np.ascontiguousarray(up[rows, :]),
            usct=np.ascontiguousarray(us[rows, :].T),
            dpk=np.ascontiguousarray(dp[:, c * 1024:(c + 1) * 1024]),
            dsct=np.ascontiguousarray(ds[:, c * 64:(c + 1) * 64].T),
            **consts))
    return in_maps


def kernel(L0_attn_x, L0_attn_Wq, L0_attn_Wk, L0_attn_Wv, L0_attn_Wo,
           L0_ffn_gate_W_packed, L0_ffn_gate_scales, L0_ffn_up_W_packed,
           L0_ffn_up_scales, L0_ffn_down_W_packed, L0_ffn_down_scales):
    x = np.ascontiguousarray(np.asarray(L0_attn_x, dtype=np.float32))
    in_maps = make_in_maps(
        x,
        np.ascontiguousarray(np.asarray(L0_attn_Wq, dtype=np.float32)),
        np.ascontiguousarray(np.asarray(L0_attn_Wk, dtype=np.float32)),
        np.ascontiguousarray(np.asarray(L0_attn_Wv, dtype=np.float32)),
        np.ascontiguousarray(np.asarray(L0_attn_Wo, dtype=np.float32)),
        np.asarray(L0_ffn_gate_W_packed, dtype=np.int32),
        np.asarray(L0_ffn_gate_scales, dtype=np.float32),
        np.asarray(L0_ffn_up_W_packed, dtype=np.int32),
        np.asarray(L0_ffn_up_scales, dtype=np.float32),
        np.asarray(L0_ffn_down_W_packed, dtype=np.int32),
        np.asarray(L0_ffn_down_scales, dtype=np.float32))
    nc = _get_nc()
    res = run_bass_kernel_spmd(nc, in_maps, core_ids=list(range(NCORES)))
    out = np.empty((2, 8192, D), dtype=np.float32)
    for c in range(NCORES):
        b, qs = divmod(c, 4)
        out[b, qs * 2048:(qs + 1) * 2048, :] = np.asarray(
            res.results[c]["out"], dtype=np.float32)
    return out


# version 07
# speedup vs baseline: 1.0522x; 1.0522x over previous
import numpy as np
import ml_dtypes
from contextlib import ExitStack
import concourse.bass as bass
import concourse.mybir as mybir
from concourse.bass_utils import run_bass_kernel_spmd

F32 = mybir.dt.float32
BF16 = mybir.dt.bfloat16
I32 = mybir.dt.int32
AF = mybir.ActivationFunctionType
ALU = mybir.AluOpType
AX = mybir.AxisListType

D = 2048
NHEAD = 16
NKV = 4
HD = 128
KVD = 512
TLOC = 2048
NCORES = 8
RMS_EPS = float(np.finfo(np.float32).eps)
LN_EPS = 1e-5
V_EPS2 = 1e-24
ISCALE = float(HD) ** -0.5
GC1 = 0.7978845608028654 * 0.044715
GC0 = 0.7978845608028654

DVE_OF16 = 10

_BF = ml_dtypes.bfloat16


class Ch:
    __slots__ = ("sem", "n")

    def __init__(self, sem):
        self.sem = sem
        self.n = 0

    def inc(self, k=1):
        self.n += k
        return self

    def tok(self):
        return (self.sem, self.n)


def build(dbg=()):
    nc = bass.Bass("TRN2", target_bir_lowering=False, debug=False)

    x_in = nc.dram_tensor("x_in", [TLOC, D], F32, kind="ExternalInput").ap()
    wsrc = {
        "wq": nc.dram_tensor("wq", [D, D], F32, kind="ExternalInput").ap(),
        "wk": nc.dram_tensor("wk", [KVD, D], F32, kind="ExternalInput").ap(),
        "wv": nc.dram_tensor("wv", [KVD, D], F32, kind="ExternalInput").ap(),
        "wo": nc.dram_tensor("wo", [D, D], F32, kind="ExternalInput").ap(),
    }
    gpk = nc.dram_tensor("gpk", [2048, 1024], I32, kind="ExternalInput").ap()
    gsct = nc.dram_tensor("gsct", [64, 2048], F32, kind="ExternalInput").ap()
    upk = nc.dram_tensor("upk", [2048, 1024], I32, kind="ExternalInput").ap()
    usct = nc.dram_tensor("usct", [64, 2048], F32, kind="ExternalInput").ap()
    dpk = nc.dram_tensor("dpk", [2048, 1024], I32, kind="ExternalInput").ap()
    dsct = nc.dram_tensor("dsct", [64, 2048], F32, kind="ExternalInput").ap()
    ident16 = nc.dram_tensor("ident16", [128, 128], BF16, kind="ExternalInput").ap()
    identf = nc.dram_tensor("identf", [128, 128], F32, kind="ExternalInput").ap()
    ones16 = nc.dram_tensor("ones16", [128, 128], BF16, kind="ExternalInput").ap()
    onesf = nc.dram_tensor("onesf", [128, 128], F32, kind="ExternalInput").ap()
    r4 = nc.dram_tensor("r4", [4, 128], F32, kind="ExternalInput").ap()
    out_d = nc.dram_tensor("out", [TLOC, D], F32, kind="ExternalOutput").ap()

    wT_d = {
        "wq": nc.dram_tensor("wqT_d", [D, D], BF16).ap(),
        "wk": nc.dram_tensor("wkT_d", [D, KVD], BF16).ap(),
        "wv": nc.dram_tensor("wvT_d", [D, KVD], BF16).ap(),
        "wo": nc.dram_tensor("woT_d", [D, D], BF16).ap(),
    }
    sct_d = [gsct, usct, dsct]
    kag_i = nc.dram_tensor("kag_i", [KVD, TLOC], BF16)
    kag_o = nc.dram_tensor("kag_o", [4 * KVD, TLOC], BF16)
    vag_i = nc.dram_tensor("vag_i", [TLOC, KVD], BF16)
    vag_o = nc.dram_tensor("vag_o", [4 * TLOC, KVD], BF16)
    gag_i = nc.dram_tensor("gag_i", [D, 2048], BF16)
    gag_o = nc.dram_tensor("gag_o", [NCORES * D, 2048], BF16)
    uag_i = nc.dram_tensor("uag_i", [D, 2048], BF16)
    uag_o = nc.dram_tensor("uag_o", [NCORES * D, 2048], BF16)
    dag_i = nc.dram_tensor("dag_i", [2048, D], BF16)
    dag_o = nc.dram_tensor("dag_o", [NCORES * 2048, D], BF16)
    x1_d = nc.dram_tensor("x1_d", [TLOC, D], F32).ap()

    dbg_d = {}
    for name, shape, dt in dbg:
        dbg_d[name] = nc.dram_tensor("dbg_" + name, shape, dt,
                                     kind="ExternalOutput").ap()

    EV = []

    def E(engine, fn):
        EV.append((engine, fn))

    def ew(engine, *toks):
        toks = [t for t in toks if t is not None and t[1] > 0]
        if toks:
            def f(e, toks=tuple(toks)):
                for s, v in toks:
                    e.wait_ge(s, v)
            E(engine, f)

    with ExitStack() as ctx:
        def sb(name, shape, dtype):
            return ctx.enter_context(nc.sbuf_tensor(name, shape, dtype))

        def psum(name, shape, dtype):
            return ctx.enter_context(nc.psum_tensor(name, shape, dtype))

        def sem(name):
            return Ch(ctx.enter_context(nc.semaphore(name)))

        arena_a = sb("arena_a", [128, 32768], BF16)
        arena_b = sb("arena_b", [128, 32768], BF16)
        arena_c = sb("arena_c", [128, 8192], F32)
        vnT = sb("vnT", [128, 8192], BF16)
        stag = sb("stag", [128, 4096], F32)
        wogrn = sb("wogrn", [128, 4096], BF16)
        kstg = sb("kstg", [128, 512], BF16)
        stats = sb("stats", [128, 16], F32)
        epsb = sb("epsb", [128, 4], F32)
        sid16 = sb("sid16", [128, 128], BF16)
        sidf = sb("sidf", [128, 128], F32)
        son16 = sb("son16", [128, 128], BF16)
        sonf = sb("sonf", [128, 128], F32)
        sr4 = sb("sr4", [4, 128], F32)
        sct_t = sb("sct_t", [4, 1024], F32)

        wrow16 = arena_b[:, 16384:18432]
        wstg16 = arena_b[:, 18432:20480]
        vts = wstg16
        vstg = kstg
        sq5 = wogrn.bitcast(F32)[:, 0:1024]
        st5 = wogrn.bitcast(F32)[:, 1024:2048]
        xn16_p1 = wogrn[:, 0:2048]
        xn16_p4 = vnT[:, 4096:6144]
        p4st = vnT.bitcast(F32)

        cF = arena_c
        t_e = cF[:, 0:2048].bitcast(I32)
        t_s2 = cF[:, 2048:4096].bitcast(I32)
        t_a2 = cF[:, 4096:6144]
        spk_v = cF[:, 6144:7168].bitcast(I32)
        vals_v = cF[:, 6144:7168].bitcast(BF16)
        svalT = cF[:, 7168:7424].bitcast(BF16)
        p1st = cF[:, 7424:7808]
        srstd = stag[:, 0:1024]
        sqrms = stag[:, 1024:1280].bitcast(BF16)
        acc_d = cF[:, 0:1024]
        acc_p = cF[:, 1024:2048]
        sp = [cF[:, 2048:2560].bitcast(BF16), cF[:, 2560:3072].bitcast(BF16),
              cF[:, 6656:7168].bitcast(BF16), cF[:, 7168:7680].bitcast(BF16)]
        prodb = cF[:, 3072:3584].bitcast(BF16)
        tmpc = cF[:, 3584:4608]
        recip = cF[:, 4608:5632]
        ytmp = cF[:, 5632:6656]
        outT = [cF[:, j * 512:(j + 1) * 512] for j in range(16)]
        accF = outT

        def xnT(jc):
            return arena_a[:, jc * 2048:(jc + 1) * 2048]
        xn1T = xnT

        def ktb(buf):
            return arena_a[:, buf * 8192:(buf + 1) * 8192]

        def vvb(buf):
            return arena_a[:, 16384 + buf * 8192:16384 + (buf + 1) * 8192]

        def qyb(h):
            return arena_b[:, h * 2048:(h + 1) * 2048]

        def interb(buf, jn):
            return arena_b[:, buf * 8192 + jn * 512:buf * 8192 + (jn + 1) * 512]

        def gslab(buf):
            return arena_b[:, 16384 + buf * 4096:16384 + (buf + 1) * 4096]

        def uslab(buf):
            return arena_b[:, 24576 + buf * 4096:24576 + (buf + 1) * 4096]

        def dslab(buf):
            return vnT[:, buf * 4096:(buf + 1) * 4096]

        pA = psum("pA", [128, 1024], F32)
        pB = psum("pB", [128, 1024], F32)
        pC = psum("pC", [128, 1024], F32)
        pD = psum("pD", [128, 1024], F32)
        pA16 = pA.bitcast(BF16)
        pB16 = pB.bitcast(BF16)
        pC16 = pC.bitcast(BF16)
        pD16 = pD.bitcast(BF16)
        accslot = [pA[:, 0:512], pA[:, 512:1024], pB[:, 0:512], pB[:, 512:1024]]

        ldq = sem("ldq")
        ldx = sem("ldx")
        ldp = sem("ldp")
        stq = sem("stq")
        c_pk = sem("c_pk")
        c_sct = [sem("c_sct0"), sem("c_sct1")]
        c_wrow = sem("c_wrow")
        c_awst = sem("c_awst")
        c_grn = [sem("c_grn0"), sem("c_grn1")]
        c_x = [sem("c_x0"), sem("c_x1")]
        c_kld = [sem("c_kld0"), sem("c_kld1")]
        c_vld = [sem("c_vld0"), sem("c_vld1")]
        c_kst = sem("c_kst")
        c_vst = sem("c_vst")
        c_gsl = [sem("c_gsl0"), sem("c_gsl1")]
        c_usl = [sem("c_usl0"), sem("c_usl1")]
        c_dsl = [sem("c_dsl0"), sem("c_dsl1")]
        c_x1 = sem("c_x1s")
        c_dbg = sem("c_dbg")
        pe = sem("pe")
        dv = sem("dv")
        ac = sem("ac")
        pl = sem("pl")
        cck = sem("cck")
        ccv = sem("ccv")
        ccw = [sem("ccg"), sem("ccu"), sem("ccd")]

        def dma(queue, dst, src, ch):
            E(queue, lambda e, d=dst, s=src, c=ch.inc(16): e.dma_start(
                out=d, in_=s).then_inc(c.sem, 16))
            return ch.tok()

        wT_r = {k: v.rearrange("(jc p) o -> p jc o", p=128)
                for k, v in wT_d.items()}

        def mm(acc_ap, lhsT_ap, rhs_ap, start, stop, mark=False):
            if mark:
                c = pe.inc()
                E("tensor", lambda e, a=acc_ap, l=lhsT_ap, r=rhs_ap, s1=start,
                  s2=stop, c=c: e.matmul(a, l, r, start=s1, stop=s2)
                  .then_inc(c.sem, 1))
            else:
                E("tensor", lambda e, a=acc_ap, l=lhsT_ap, r=rhs_ap, s1=start,
                  s2=stop: e.matmul(a, l, r, start=s1, stop=s2))

        def tp(dst_ap, src_ap, ident, mark=False):
            if mark:
                c = pe.inc()
                E("tensor", lambda e, d=dst_ap, s=src_ap, i=ident, c=c:
                  e.transpose(d, s, i).then_inc(c.sem, 1))
            else:
                E("tensor", lambda e, d=dst_ap, s=src_ap, i=ident:
                  e.transpose(d, s, i))

        deq_state = dict(spk_free=None, pd_free=None, sval_free=None,
                         wrow_free=None)
        shard_toks = [None, None, None]
        deq_queue = [(w, nt) for w in range(3) for nt in range(16)]
        pk_srcs = [gpk, upk, dpk]
        ag_is = [gag_i, uag_i, dag_i]

        def emit_decode_tile(job):
            wi, nt = job
            s = deq_state
            dmp_t = None
            ew("sync", s["spk_free"])
            pk_tok = dma("sync", spk_v[:],
                         pk_srcs[wi][nt * 128:(nt + 1) * 128, :], c_pk)
            if wi == 0 and nt == 0 and "dpk" in dbg_d:
                ew("scalar", pk_tok)
                pkd = dma("scalar", dbg_d["dpk"].bitcast(I32), spk_v[:], c_dbg)
                ew("vector", pkd)
            ew("vector", pk_tok, s["wrow_free"], consts_tok)
            def dvstep(fn):
                c = dv.inc()
                E("vector", lambda e, fn=fn, c=c: fn(e).then_inc(c.sem, 1))
                ew("vector", dv.tok())
            i2 = lambda ap: ap.rearrange("p (c two) -> p c two", two=2)
            dvstep(lambda e: e.tensor_scalar(
                out=i2(t_s2)[:, :, 0], in0=spk_v[:], scalar1=8, scalar2=28,
                op0=ALU.bitwise_and, op1=ALU.logical_shift_left))
            dvstep(lambda e: e.tensor_scalar(
                out=i2(t_s2)[:, :, 1], in0=spk_v[:], scalar1=0x80, scalar2=24,
                op0=ALU.bitwise_and, op1=ALU.logical_shift_left))
            dvstep(lambda e: e.tensor_scalar(
                out=i2(t_e)[:, :, 0], in0=spk_v[:], scalar1=7, scalar2=22,
                op0=ALU.bitwise_and, op1=ALU.logical_shift_left))
            dvstep(lambda e: e.tensor_scalar(
                out=i2(t_e)[:, :, 1], in0=spk_v[:], scalar1=0x70, scalar2=18,
                op0=ALU.bitwise_and, op1=ALU.logical_shift_left))
            dvstep(lambda e: e.tensor_scalar(
                out=t_e[:], in0=t_e[:], scalar1=0x40000000, scalar2=None,
                op0=ALU.add))
            dvstep(lambda e: e.tensor_scalar(
                out=t_a2[:], in0=t_e[:].bitcast(F32), scalar1=2.0,
                scalar2=-4.0, op0=ALU.mult, op1=ALU.add))
            dvstep(lambda e: e.tensor_tensor(
                out=t_a2[:], in0=t_e[:].bitcast(F32), in1=t_a2[:],
                op=ALU.min))
            dvstep(lambda e: e.tensor_scalar(
                out=t_s2[:], in0=t_s2[:], scalar1=0x3F800000, scalar2=None,
                op0=ALU.add))
            dvstep(lambda e: e.tensor_tensor(
                out=vals_v[:], in0=t_a2[:], in1=t_s2[:].bitcast(F32),
                op=ALU.mult))
            dec_tok = dv.tok()
            if wi == 0 and nt == 1 and "dvals1" in dbg_d:
                ew("scalar", dec_tok)
                s["spk_free"] = dma("scalar", dbg_d["dvals1"], vals_v[:], c_dbg)
            if wi == 0 and nt == 0 and "dte" in dbg_d:
                ew("scalar", dec_tok)
                ted = dma("scalar", dbg_d["dte"].bitcast(I32), t_e[:], c_dbg)
                tad = dma("scalar", dbg_d["dta"], t_a2[:], c_dbg)
                tsd = dma("scalar", dbg_d["dts"].bitcast(I32), t_s2[:], c_dbg)
                ew("tensor", tsd)
            if wi == 0 and nt == 0 and "dvals" in dbg_d:
                ew("scalar", dec_tok)
                s["spk_free"] = dma("scalar", dbg_d["dvals"], vals_v[:], c_dbg)
            for g in range(4):
                sb_ = g % 2
                ew("sync", s.get("sct_free%d" % sb_))
                sct_tok = dma(
                    "sync", sct_t[:, sb_ * 512:(sb_ + 1) * 512],
                    sct_d[wi].rearrange("(jc p) n -> p jc n", p=4)
                    [:, 4 * g:4 * g + 4, nt * 128:(nt + 1) * 128], c_sct[sb_])
                ew("tensor", dec_tok if g == 0 else None, sct_tok,
                   s["pd_free"], s["sval_free"])
                for k in range(4):
                    jc = g * 4 + k
                    tp(pD16[:, k * 128:(k + 1) * 128],
                       vals_v[:, jc * 128:(jc + 1) * 128], sid16[:])
                    mm(pD[:, 512 + k * 128:512 + (k + 1) * 128], sr4[:],
                       sct_t[:, sb_ * 512 + k * 128:sb_ * 512 + (k + 1) * 128],
                       True, True, mark=(k == 3))
                s["sct_free%d" % sb_] = (pe.sem, pe.n)
                grp_tok = pe.tok()
                ew("scalar", grp_tok, s["wrow_free"] if g == 0 else None)
                E("scalar", lambda e, c=ac.inc(): e.copy(
                    out=svalT[:], in_=pD16[:, 0:512]).then_inc(c.sem, 1))
                evt = ac.tok()
                ew("vector", evt)
                E("vector", lambda e, g=g, c=dv.inc(): e.tensor_tensor(
                    out=wrow16[:, g * 512:(g + 1) * 512], in0=svalT[:],
                    in1=pD[:, 512:1024], op=ALU.mult).then_inc(c.sem, 1))
                s["sval_free"] = dv.tok()
                s["pd_free"] = dv.tok()
            if not (wi == 0 and nt == 0 and "dvals" in dbg_d):
                s["spk_free"] = pe.tok()
            wrow_tok = dv.tok()
            if wi == 0 and nt == 1 and "dwrow1" in dbg_d:
                ew("scalar", wrow_tok)
                dmp_t = dma("scalar", dbg_d["dwrow1"], wrow16[:], c_dbg)
                s["spk_free"] = dmp_t
            if wi == 0 and nt == 0 and "dwrow" in dbg_d:
                ew("scalar", wrow_tok)
                dmp_t = dma("scalar", dbg_d["dwrow"], wrow16[:], c_dbg)
            ew("sync", wrow_tok,
               dmp_t if (wi == 0 and nt == 0 and "dwrow" in dbg_d) else None)
            s["wrow_free"] = dma(
                "sync",
                ag_is[wi].ap().rearrange("(jc p) n -> p jc n", p=128)
                [:, :, nt * 128:(nt + 1) * 128],
                wrow16[:].rearrange("p (jc r) -> p jc r", r=128), c_wrow)
            if nt == 15:
                shard_toks[wi] = c_wrow.tok()

        def pump(n=1):
            for _ in range(n):
                if deq_queue:
                    emit_decode_tile(deq_queue.pop(0))

        for dst, src in ((sid16, ident16), (sidf, identf), (son16, ones16),
                         (sonf, onesf), (sr4, r4)):
            dma("scalar", dst[:], src, ldx)
        consts_tok = ldx.tok()
        E("vector", lambda e: e.memset(epsb[:, 0:1], LN_EPS))
        E("vector", lambda e: e.memset(epsb[:, 1:2], RMS_EPS))
        E("vector", lambda e: e.memset(epsb[:, 2:3], V_EPS2))

        xtile = [stag[:, 0:2048], stag[:, 2048:4096]]
        xload_free = [None, None]

        ln_state = dict(xn_free=None)

        def ln_chain2(src_tile, ln_loaded, xb, tglob, pm, pq, sA, sB, sC,
                      tr_wait, ps_wait):
            dst3 = arena_a[:].rearrange("p (jc r) -> p jc r", r=2048)[
                :, :, tglob * 128:(tglob + 1) * 128]
            ew("vector", ln_loaded, consts_tok, ln_state["xn_free"])
            E("vector", lambda e, t=src_tile, xb=xb, c=dv.inc():
              e.tensor_copy(out=xb, in_=t).then_inc(c.sem, 1))
            cast_tok = dv.tok()
            ew("tensor", cast_tok, *tr_wait)
            for jc in range(16):
                tp(pA16[:, jc * 128:(jc + 1) * 128],
                   xb[:, jc * 128:(jc + 1) * 128], sid16[:], mark=(jc == 15))
            tr_tok = pe.tok()
            ew("scalar", tr_tok)
            E("scalar", lambda e, d=dst3, c=ac.inc(): e.copy(
                out=d, in_=pA16[:, 0:2048].rearrange(
                    "p (jc r) -> p jc r", r=128)).then_inc(c.sem, 1))
            evac_tok = ac.tok()
            ew("scalar", evac_tok)
            E("scalar", lambda e, d=dst3, xb=xb, c=ac.inc(): e.activation(
                out=xb, in_=d, func=AF.Square).then_inc(c.sem, 1))
            sqa_tok = ac.tok()
            ew("tensor", sqa_tok, *ps_wait)
            for jc in range(16):
                mm(pm, son16[:],
                   arena_a[:, jc * 2048 + tglob * 128:
                           jc * 2048 + tglob * 128 + 128],
                   jc == 0, jc == 15)
            for jc in range(16):
                mm(pq, son16[:], xb[:, jc * 128:(jc + 1) * 128],
                   jc == 0, jc == 15, mark=(jc == 15))
            smm_tok = pe.tok()
            ew("vector", smm_tok)
            E("vector", lambda e, pm=pm, sA=sA: e.tensor_scalar(
                out=sA, in0=pm, scalar1=1.0 / D, scalar2=None, op0=ALU.mult))
            E("vector", lambda e, sA=sA, sB=sB: e.tensor_tensor(
                out=sB, in0=sA, in1=sA, op=ALU.mult))
            E("vector", lambda e, pq=pq, sB=sB, c=dv.inc():
              e.scalar_tensor_tensor(
                  out=sB, in0=pq, scalar=1.0 / D, in1=sB,
                  op0=ALU.mult, op1=ALU.subtract).then_inc(c.sem, 1))
            var_tok = dv.tok()
            ew("scalar", var_tok)
            E("scalar", lambda e, sB=sB, sC=sC, c=ac.inc(): e.activation(
                out=sC, in_=sB, func=AF.Ln, bias=epsb[:, 0:1], scale=1.0)
              .then_inc(c.sem, 1))
            ln_tok = ac.tok()
            ew("scalar", ln_tok)
            E("scalar", lambda e, sC=sC, c=ac.inc(): e.activation(
                out=sC, in_=sC, func=AF.Exp, scale=-0.5).then_inc(c.sem, 1))
            exp_tok = ac.tok()
            ew("vector", exp_tok)
            E("vector", lambda e, d=dst3, sA=sA: e.tensor_tensor(
                out=d, in0=d, in1=sA.unsqueeze(1).broadcast_to([128, 16, 128]),
                op=ALU.subtract))
            E("vector", lambda e, d=dst3, sC=sC, c=dv.inc(): e.tensor_tensor(
                out=d, in0=d, in1=sC.unsqueeze(1).broadcast_to([128, 16, 128]),
                op=ALU.mult).then_inc(c.sem, 1))
            done_tok = dv.tok()
            ln_state["xn_free"] = smm_tok
            return done_tok, evac_tok, smm_tok, var_tok

        pa_free = None
        ps_free = None
        for tt in range(16):
            ew("scalar", xload_free[tt % 2])
            x_tok = dma("scalar", xtile[tt % 2],
                        x_in[tt * 128:(tt + 1) * 128, :], c_x[tt % 2])
            xn_tok, ev_t, smm_t, st_t = ln_chain2(
                xtile[tt % 2], x_tok, xn16_p1, tt,
                pC[:, 0:128], pC[:, 512:640],
                p1st[:, 0:128], p1st[:, 128:256], p1st[:, 256:384],
                tr_wait=(pa_free,), ps_wait=(ps_free,))
            xload_free[tt % 2] = xn_tok
            pa_free = ev_t
            ps_free = st_t
        xnT_all = dv.tok()
        if "xn" in dbg_d:
            ew("scalar", xnT_all)
            dma("scalar", dbg_d["xn"], arena_a[:], c_dbg)

        aw_state = dict(pb_free=None, wst_free=None)
        wT_written = {}

        def aw_prep(wname, ntile, first=None):
            for ot in range(ntile):
                ew("gpsimd", first if ot == 0 else None,
                   aw_state["wst_free"])
                wl_tok = dma("gpsimd", wstg16[:],
                             wsrc[wname][ot * 128:(ot + 1) * 128, :], ldp)
                ew("tensor", wl_tok, aw_state["pb_free"])
                for jc in range(16):
                    tp(pB16[:, jc * 128:(jc + 1) * 128],
                       wstg16[:, jc * 128:(jc + 1) * 128], sid16[:],
                       mark=(jc == 15))
                tr_tok = pe.tok()
                ew("vector", tr_tok)
                E("vector", lambda e, c=dv.inc(): e.tensor_copy(
                    out=wstg16[:], in_=pB16[:, 0:2048]).then_inc(c.sem, 1))
                ev_tok = dv.tok()
                aw_state["pb_free"] = ev_tok
                ew("sync", ev_tok)
                aw_state["wst_free"] = dma(
                    "sync", wT_r[wname][:, :, ot * 128:(ot + 1) * 128],
                    wstg16[:].rearrange("p (jc r) -> p jc r", r=128), c_awst)
                pump(1)
            wT_written[wname] = c_awst.tok()

        aw_prep("wk", 4)
        aw_evac_all = dv.tok()

        p2 = dict(slot=0, acc_free=[None] * 4, ms_free=ps_free)
        vts_done = [None] * 4
        vts_state = dict(free=None)
        kst_free = [None]
        vst_free = [None]
        pcV_free = [None]
        qproj_done = [None] * NHEAD
        grn_free = [None, None]

        def proj_tile(stat_fn, tq, kind, dest_jo, extra=()):
            sl = p2["slot"] % 4
            p2["slot"] += 1
            acc = accslot[sl]
            ew("tensor", *extra, p2["acc_free"][sl])
            for jc in range(16):
                mm(acc, stat_fn(jc), xnT(jc)[:, tq * 512:(tq + 1) * 512],
                   jc == 0, jc == 15, mark=(jc == 15))
            mm_tok = pe.tok()
            ew("scalar", mm_tok)
            E("scalar", lambda e, acc=acc, c=ac.inc(): e.activation(
                out=sqrms[:], in_=acc, func=AF.Square).then_inc(c.sem, 1))
            sq_tok = ac.tok()
            ew("tensor", sq_tok, p2["ms_free"])
            mm(pC[:, 0:512], son16[:], sqrms[:], True, True, mark=True)
            ms_tok = pe.tok()
            ew("scalar", ms_tok)
            if kind in ("q", "k"):
                E("scalar", lambda e: e.activation(
                    out=srstd[:, 0:512], in_=pC[:, 0:512], func=AF.Ln,
                    bias=epsb[:, 1:2], scale=1.0 / 128))
            else:
                E("scalar", lambda e: e.activation(
                    out=srstd[:, 0:512], in_=pC[:, 0:512], func=AF.Ln,
                    bias=epsb[:, 2:3], scale=1.0))
            E("scalar", lambda e, c=ac.inc(): e.activation(
                out=srstd[:, 512:1024], in_=srstd[:, 0:512], func=AF.Exp,
                scale=-0.5).then_inc(c.sem, 1))
            rstd_tok = ac.tok()
            p2["ms_free"] = rstd_tok
            if kind == "q":
                ew("vector", rstd_tok,
                   shard_toks[2] if (dest_jo == 8 and tq == 0
                                     and shard_toks[2]) else None)
                E("vector", lambda e, jo=dest_jo, tq=tq, acc=acc, c=dv.inc():
                  e.tensor_tensor(
                      out=qyb(jo)[:, tq * 512:(tq + 1) * 512], in0=acc,
                      in1=srstd[:, 512:1024], op=ALU.mult).then_inc(c.sem, 1))
                p2["acc_free"][sl] = dv.tok()
                qproj_done[dest_jo] = dv.tok()
            elif kind == "k":
                ew("vector", rstd_tok, kst_free[0])
                E("vector", lambda e, acc=acc, c=dv.inc(): e.tensor_tensor(
                    out=kstg[:], in0=acc, in1=srstd[:, 512:1024],
                    op=ALU.mult).then_inc(c.sem, 1))
                k_tok = dv.tok()
                p2["acc_free"][sl] = k_tok
                ew("scalar", k_tok)
                kst_free[0] = dma(
                    "scalar",
                    kag_i.ap()[dest_jo * 128:(dest_jo + 1) * 128,
                               tq * 512:(tq + 1) * 512], kstg[:], c_kst)
            else:
                ew("vector", rstd_tok, vts_state["free"],
                   wT_written["wv"] if vts_state["free"] is None else None)
                E("vector", lambda e, jo=dest_jo, tq=tq, acc=acc:
                  e.tensor_tensor(
                      out=vnT[:, jo * 2048 + tq * 512:
                              jo * 2048 + (tq + 1) * 512],
                      in0=acc, in1=srstd[:, 512:1024], op=ALU.mult))
                E("vector", lambda e, jo=dest_jo, acc=acc, c=dv.inc():
                  e.tensor_copy(
                      out=vts[:, jo * 512:(jo + 1) * 512], in_=acc)
                  .then_inc(c.sem, 1))
                p2["acc_free"][sl] = dv.tok()
                vts_done[dest_jo] = dv.tok()

        def load_grn(wname, jo, extra=()):
            gb = jo % 2
            ew("scalar", *extra, grn_free[gb])
            return gb, dma(
                "scalar",
                wogrn[:, gb * 2048:(gb + 1) * 2048].rearrange(
                    "p (jc r) -> p jc r", r=128),
                wT_r[wname][:, :, jo * 128:(jo + 1) * 128], c_grn[gb])

        for jo in range(4):
            gb, g_tok = load_grn(
                "wk", jo,
                extra=(wT_written["wk"], xnT_all) if jo == 0 else ())
            for tq in range(4):
                proj_tile(
                    lambda jc, gb=gb: wogrn[:, gb * 2048 + jc * 128:
                                            gb * 2048 + (jc + 1) * 128],
                    tq, "k", jo,
                    extra=((g_tok, xnT_all, aw_evac_all)
                           if (jo == 0 and tq == 0) else
                           (g_tok,) if tq == 0 else ()))
                pump(2)
            grn_free[gb] = pe.tok()

        aw_prep("wv", 4)
        for tq in range(4):
            for jo in range(4):
                gb, g_tok = load_grn(
                    "wv", jo,
                    extra=(wT_written["wv"],) if (tq == 0 and jo == 0) else ())
                proj_tile(
                    lambda jc, gb=gb: wogrn[:, gb * 2048 + jc * 128:
                                            gb * 2048 + (jc + 1) * 128],
                    tq, "v", jo, extra=(g_tok,))
                grn_free[gb] = pe.tok()
            for tc in range(4):
                tglob = tq * 4 + tc
                ew("tensor", vts_done[3], pcV_free[0])
                for jo in range(4):
                    tp(pC16[:, 1024 + jo * 128:1024 + (jo + 1) * 128],
                       vts[:, jo * 512 + tc * 128:jo * 512 + (tc + 1) * 128],
                       sid16[:], mark=(jo == 3))
                vt_tok = pe.tok()
                ew("scalar", vt_tok, vst_free[0],
                   kst_free[0] if tglob == 0 else None)
                E("scalar", lambda e, c=ac.inc(): e.copy(
                    out=vstg[:], in_=pC16[:, 1024:1536]).then_inc(c.sem, 1))
                ev_tok = ac.tok()
                pcV_free[0] = ev_tok
                ew("scalar", ev_tok)
                vst_free[0] = dma(
                    "scalar", vag_i.ap()[tglob * 128:(tglob + 1) * 128, :],
                    vstg[:], c_vst)
            vts_state["free"] = pe.tok()
            pump(2)
        kag_done = kst_free[0]
        vag_done = vst_free[0]
        v_mm_done = pe.tok()

        GROUPS = [[0, 1, 2, 3], [4, 5, 6, 7]]
        ew("gpsimd", kag_done)
        E("gpsimd", lambda e, c=cck.inc(): e.collective_compute(
            "AllGather", ALU.bypass, replica_groups=GROUPS,
            ins=[kag_i.ap().opt()], outs=[kag_o.ap().opt()]).then_inc(c.sem, 1))
        ew("gpsimd", vag_done)
        E("gpsimd", lambda e, c=ccv.inc(): e.collective_compute(
            "AllGather", ALU.bypass, replica_groups=GROUPS,
            ins=[vag_i.ap().opt()], outs=[vag_o.ap().opt()]).then_inc(c.sem, 1))

        aw_prep("wq", 16, first=pe.tok())
        aw_prep("wo", 16)
        for jo in range(16):
            gb, g_tok = load_grn(
                "wq", jo,
                extra=((wT_written["wq"], v_mm_done) if jo == 0 else ()))
            for tq in range(4):
                proj_tile(
                    lambda jc, gb=gb: wogrn[:, gb * 2048 + jc * 128:
                                            gb * 2048 + (jc + 1) * 128],
                    tq, "q", jo, extra=(g_tok,) if tq == 0 else ())
                pump(1)
            grn_free[gb] = pe.tok()
        qproj_all = dv.tok()
        p2_pe_done = pe.tok()
        pump(64)

        for wi in range(3):
            ew("gpsimd", shard_toks[wi])
            ag_o2 = [gag_o, uag_o, dag_o][wi]
            ag_i2 = ag_is[wi]
            E("gpsimd", lambda e, ag_i2=ag_i2, ag_o2=ag_o2, c=ccw[wi].inc():
              e.collective_compute(
                  "AllGather", ALU.bypass,
                  replica_groups=[list(range(NCORES))],
                  ins=[ag_i2.ap().opt()], outs=[ag_o2.ap().opt()])
              .then_inc(c.sem, 1))

        if "qy" in dbg_d:
            ew("scalar", qproj_all)
            dma("scalar", dbg_d["qy"], arena_b[:, 0:32768], c_dbg)
        if "vn" in dbg_d:
            ew("scalar", qproj_all)
            dma("scalar", dbg_d["vn"], vnT[:], c_dbg)
        if "gag" in dbg_d:
            ew("scalar", ccw[0].tok())
            dma("scalar", dbg_d["gag"], gag_o.ap(), stq)
        if "dag" in dbg_d:
            ew("scalar", ccw[2].tok())
            dma("scalar", dbg_d["dag"], dag_o.ap(), stq)
        if "kag" in dbg_d:
            ew("scalar", cck.tok())
            dma("scalar", dbg_d["kag"], kag_o.ap(), stq)
        if "vag" in dbg_d:
            ew("scalar", ccv.tok())
            dma("scalar", dbg_d["vag"], vag_o.ap(), stq)

        pS2 = [pA, pB]
        kld = [None, None]
        vld = [None, None]
        kuse = [None, None]
        vuse = [None, None]
        pS_free = [None, None]
        sp_free = [None] * 4
        pB_drain = [None, None]
        exp_tok_q = [None] * 128
        pC_drain = [None, None]
        pD_drain = [None, None]
        y_done = [[None] * 4 for _ in range(NHEAD)]

        for g in range(NKV):
            kb = g % 2
            ew("scalar", cck.tok() if g == 0 else None,
               p2_pe_done if g == 0 else None, kuse[kb])
            for rb in range(4):
                kld[kb] = dma(
                    "scalar", ktb(kb)[:, rb * 2048:(rb + 1) * 2048],
                    kag_o.ap()[rb * KVD + g * 128:rb * KVD + (g + 1) * 128, :],
                    c_kld[kb])
            ew("scalar", ccv.tok() if g == 0 else None, vuse[kb])
            vld[kb] = dma(
                "scalar", vvb(kb)[:],
                vag_o.ap().rearrange("(j p) d -> p j d", p=128)
                [:, :, g * 128:(g + 1) * 128], c_vld[kb])
            for hp in range(2):
                hA = 4 * g + 2 * hp
                for tq in range(4):
                    qsl = slice(tq * 512, (tq + 1) * 512)
                    for j in range(65):
                        if j < 64:
                            for hi in range(2):
                                h = hA + hi
                                ew("tensor",
                                   kld[kb] if (hp == 0 and tq == 0 and j == 0
                                               and hi == 0) else None,
                                   qproj_done[h] if j == 0 else None,
                                   qproj_all if (g == 0 and hp == 0 and tq == 0
                                                 and j == 0 and hi == 0)
                                   else None,
                                   pS_free[j % 2] if hi == 0 else None)
                                mm(pS2[j % 2][:, hi * 512:(hi + 1) * 512],
                                   ktb(kb)[:, j * 128:(j + 1) * 128],
                                   qyb(h)[:, qsl], True, True, mark=(hi == 1))
                            sc_tok = pe.tok()
                            kuse[kb] = sc_tok
                            sps = j % 4
                            ew("scalar", sc_tok, sp_free[sps])
                            E("scalar", lambda e, j=j, sps=sps, c=ac.inc():
                              e.activation(out=sp[sps][:], in_=pS2[j % 2][:],
                                           func=AF.Exp, scale=ISCALE)
                              .then_inc(c.sem, 1))
                            pS_free[j % 2] = ac.tok()
                            exp_tok_q[j] = ac.tok()
                        for hi in range(2):
                            if j > 0:
                                jj = j - 1
                                sps = jj % 4
                                spv = sp[sps][:, hi * 512:(hi + 1) * 512]
                                ew("tensor", exp_tok_q[jj] if hi == 0 else None,
                                   vld[kb] if (hp == 0 and tq == 0 and jj == 0
                                               and hi == 0) else None,
                                   pC_drain[hi] if jj == 0 else None,
                                   pD_drain[hi] if jj == 0 else None)
                                mm(pC[:, hi * 512:(hi + 1) * 512],
                                   vvb(kb)[:, jj * 128:(jj + 1) * 128],
                                   spv, jj == 0, jj == 63)
                                mm(pD[:, hi * 512:(hi + 1) * 512], son16[:],
                                   spv, jj == 0, jj == 63, mark=(hi == 1))
                                pv_tok = pe.tok()
                                if hi == 1:
                                    sp_free[sps] = pv_tok
                                vuse[kb] = pv_tok
                    for hi in range(2):
                        h = hA + hi
                        hs = slice(hi * 512, (hi + 1) * 512)
                        ew("scalar", pe.tok())
                        E("scalar", lambda e, hs=hs, c=ac.inc():
                          e.activation(out=recip[:, hs], in_=pD[:, hs],
                                       func=AF.Ln, scale=1.0)
                          .then_inc(c.sem, 1))
                        ln_done = ac.tok()
                        ew("scalar", ac.tok())
                        E("scalar", lambda e, hs=hs, c=ac.inc():
                          e.activation(out=recip[:, hs], in_=recip[:, hs],
                                       func=AF.Exp, scale=-1.0)
                          .then_inc(c.sem, 1))
                        rc_tok = ac.tok()
                        ew("vector", rc_tok)
                        E("vector", lambda e, hs=hs, c=dv.inc():
                          e.tensor_tensor(
                              out=ytmp[:, hs], in0=pC[:, hs],
                              in1=recip[:, hs], op=ALU.mult)
                          .then_inc(c.sem, 1))
                        pC_drain[hi] = dv.tok()
                        E("vector", lambda e, g=g, hs=hs, qsl=qsl, c=dv.inc():
                          e.tensor_tensor(
                              out=prodb[:, hs], in0=ytmp[:, hs],
                              in1=vnT[:, g * 2048:(g + 1) * 2048][:, qsl],
                              op=ALU.mult).then_inc(c.sem, 1))
                        pr_tok = dv.tok()
                        ew("tensor", pr_tok, ln_done)
                        mm(pD[:, hs], son16[:], prodb[:, hs], True, True,
                           mark=True)
                        sr_tok = pe.tok()
                        ew("vector", sr_tok,
                           shard_toks[2] if (h == 8 and tq == 0) else None)
                        E("vector", lambda e, g=g, hs=hs, qsl=qsl, c=dv.inc():
                          e.tensor_tensor(
                              out=tmpc[:, hs], in0=pD[:, hs],
                              in1=vnT[:, g * 2048:(g + 1) * 2048][:, qsl],
                              op=ALU.mult).then_inc(c.sem, 1))
                        pD_drain[hi] = dv.tok()
                        E("vector", lambda e, h=h, hs=hs, qsl=qsl, c=dv.inc():
                          e.tensor_tensor(
                              out=qyb(h)[:, qsl], in0=ytmp[:, hs],
                              in1=tmpc[:, hs], op=ALU.subtract)
                          .then_inc(c.sem, 1))
                        y_done[h][tq] = dv.tok()
        p3_dve = dv.tok()
        p3_pool = pl.tok()
        p3_pe = pe.tok()
        if "y" in dbg_d:
            ew("scalar", y_done[NHEAD - 1][1])
            dma("scalar", dbg_d["y"], arena_b[:, 0:32768], c_dbg)

        ogrn_free = [None, None]
        oacc_free = [None, None, None, None]
        outT_free = [None] * 16
        x4_free = [None, None]
        x1st = [None]
        pc4_free = None
        pd4_free = None
        xn1_last = None

        for qb in range(4):
            jo_last = None
            for jo in range(16):
                gb = jo % 2
                ew("sync", wT_written["wo"] if (qb == 0 and jo == 0) else None,
                   qproj_all if (qb == 0 and jo == 0) else None,
                   ogrn_free[gb])
                g_tok = dma("sync",
                            wogrn[:, gb * 2048:(gb + 1) * 2048].rearrange(
                                "p (jc r) -> p jc r", r=128),
                            wT_r["wo"][:, :, jo * 128:(jo + 1) * 128],
                            c_grn[gb])
                sl = (qb * 16 + jo) % 4
                acc = accslot[sl]
                ew("tensor", g_tok, oacc_free[sl],
                   y_done[NHEAD - 1][qb] if jo == 0 else None)
                for jc in range(16):
                    mm(acc, wogrn[:, gb * 2048 + jc * 128:
                                  gb * 2048 + (jc + 1) * 128],
                       qyb(jc)[:, qb * 512:(qb + 1) * 512],
                       jc == 0, jc == 15, mark=(jc == 15))
                mm_tok = pe.tok()
                ogrn_free[gb] = mm_tok
                ew("scalar", mm_tok,
                   p3_dve if (qb == 0 and jo == 0) else None,
                   p3_pool if (qb == 0 and jo == 0) else None,
                   outT_free[jo])
                E("scalar", lambda e, jo=jo, acc=acc, c=ac.inc(): e.copy(
                    out=outT[jo], in_=acc).then_inc(c.sem, 1))
                oacc_free[sl] = ac.tok()
                jo_last = ac.tok()
            for tb in range(4):
                tglob = qb * 4 + tb
                ew("scalar", x4_free[tb % 2], x1st[0])
                xl_tok = dma("scalar", xtile[tb % 2],
                             x_in[tglob * 128:(tglob + 1) * 128, :],
                             c_x[tb % 2])
                ew("tensor", jo_last, pc4_free, pd4_free,
                   pC_drain[1] if (qb == 0 and tb == 0) else None,
                   pD_drain[1] if (qb == 0 and tb == 0) else None)
                for jo in range(16):
                    dst = pC if jo < 8 else pD
                    off = (jo % 8) * 128
                    tp(dst[:, off:off + 128],
                       outT[jo][:, tb * 128:(tb + 1) * 128], sidf[:],
                       mark=(jo == 15))
                tr_tok = pe.tok()
                if tb == 3:
                    for jo in range(16):
                        outT_free[jo] = tr_tok
                ew("vector", tr_tok, xl_tok, x1st[0])
                E("vector", lambda e, t=xtile[tb % 2]: e.tensor_tensor(
                    out=t[:, 0:1024], in0=t[:, 0:1024], in1=pC[:],
                    op=ALU.add))
                E("vector", lambda e, t=xtile[tb % 2], c=dv.inc():
                  e.tensor_tensor(
                      out=t[:, 1024:2048], in0=t[:, 1024:2048], in1=pD[:],
                      op=ALU.add).then_inc(c.sem, 1))
                x1_tok = dv.tok()
                pc4_free = x1_tok
                pd4_free = x1_tok
                ew("scalar", x1_tok)
                x1st[0] = dma("scalar",
                              x1_d[tglob * 128:(tglob + 1) * 128, :],
                              xtile[tb % 2], c_x1)
                xn_tok, ev_t, smm_t, st_t = ln_chain2(
                    xtile[tb % 2], x1_tok, xn16_p4, tglob,
                    pC[:, 0:128], pD[:, 0:128],
                    p4st[:, 0:128], p4st[:, 128:256], p4st[:, 256:384],
                    tr_wait=(p3_pe if tglob == 0 else None,),
                    ps_wait=(None,))
                x4_free[tb % 2] = xn_tok
                pc4_free = st_t
                pd4_free = st_t
                xn1_last = xn_tok
                oacc_free[0] = smm_t
                oacc_free[1] = smm_t
        xn1T_all = xn1_last
        x1_all = x1st[0]
        p4_pe_done = pe.tok()
        if "x1" in dbg_d:
            ew("scalar", x1_all)
            dma("scalar", dbg_d["x1"], x1_d, stq)

        gag_r = gag_o.ap().rearrange("(r jc p) n -> p r jc n", p=128, jc=16)
        uag_r = uag_o.ap().rearrange("(r jc p) n -> p r jc n", p=128, jc=16)
        dag_r = dag_o.ap().rearrange("(r jn p) h -> p r jn h", p=128, jn=16)
        gsl_free = [None, None]
        usl_free = [None, None]
        dsl_free = [None, None]
        int_free = [None, None]
        gu_free = [None, None]
        dw_free = [None, None]
        acc5_tok = [None] * 16
        scr_free = [None, None]
        x15_free = [None, None]
        out5_free = [None]
        fin_ps_free = None

        for tq4 in range(4):
            for r in range(NCORES):
                it = r % 2
                int_done = None
                for grp in range(8):
                    gb = grp % 2
                    first5 = tq4 == 0 and r == 0 and grp == 0
                    ew("sync",
                       ccw[0].tok() if first5 else None,
                       ccw[1].tok() if first5 else None,
                       p4_pe_done if first5 else None,
                       gsl_free[gb])
                    gt = dma("sync",
                             gslab(gb).rearrange("p (jc r_) -> p jc r_",
                                                 r_=256),
                             gag_r[:, r, :, grp * 256:(grp + 1) * 256],
                             c_gsl[gb])
                    ew("sync", usl_free[gb])
                    ut = dma("sync",
                             uslab(gb).rearrange("p (jc r_) -> p jc r_",
                                                 r_=256),
                             uag_r[:, r, :, grp * 256:(grp + 1) * 256],
                             c_usl[gb])
                    gps, ups = ((pA, pB), (pC, pD))[gb]
                    ew("tensor", gt, ut, gu_free[gb],
                       xn1T_all if first5 else None)
                    for jc in range(16):
                        for k in range(2):
                            mm(gps[:, k * 512:(k + 1) * 512],
                               gslab(gb)[:, jc * 256 + k * 128:
                                         jc * 256 + (k + 1) * 128],
                               xn1T(jc)[:, tq4 * 512:(tq4 + 1) * 512],
                               jc == 0, jc == 15)
                        for k in range(2):
                            mm(ups[:, k * 512:(k + 1) * 512],
                               uslab(gb)[:, jc * 256 + k * 128:
                                         jc * 256 + (k + 1) * 128],
                               xn1T(jc)[:, tq4 * 512:(tq4 + 1) * 512],
                               jc == 0, jc == 15,
                               mark=(jc == 15 and k == 1))
                    gu_tok = pe.tok()
                    gsl_free[gb] = gu_tok
                    usl_free[gb] = gu_tok
                    for k in range(2):
                        jn = grp * 2 + k
                        sc = k
                        A = sq5[:, sc * 512:(sc + 1) * 512]
                        B = st5[:, sc * 512:(sc + 1) * 512]
                        gp = gps[:, k * 512:(k + 1) * 512]
                        up = ups[:, k * 512:(k + 1) * 512]
                        ew("scalar", gu_tok, scr_free[sc])
                        E("scalar", lambda e, gp=gp, A=A, c=ac.inc():
                          e.activation(out=A, in_=gp, func=AF.Square)
                          .then_inc(c.sem, 1))
                        sq_tok = ac.tok()
                        ew("vector", sq_tok,
                           int_free[it] if jn == 0 else None)
                        E("vector", lambda e, A=A, B=B: e.tensor_scalar(
                            out=B, in0=A, scalar1=GC1, scalar2=GC0,
                            op0=ALU.mult, op1=ALU.add))
                        E("vector", lambda e, gp=gp, A=A, B=B, c=dv.inc():
                          e.tensor_tensor(out=A, in0=B, in1=gp, op=ALU.mult)
                          .then_inc(c.sem, 1))
                        arg_tok = dv.tok()
                        ew("scalar", arg_tok)
                        E("scalar", lambda e, A=A, B=B, c=ac.inc():
                          e.activation(out=B, in_=A, func=AF.Tanh)
                          .then_inc(c.sem, 1))
                        th_tok = ac.tok()
                        ew("vector", th_tok)
                        E("vector", lambda e, B=B: e.tensor_scalar(
                            out=B, in0=B, scalar1=0.5, scalar2=0.5,
                            op0=ALU.mult, op1=ALU.add))
                        E("vector", lambda e, gp=gp, A=A, B=B:
                          e.tensor_tensor(out=A, in0=B, in1=gp, op=ALU.mult))
                        E("vector", lambda e, up=up, A=A, it=it, jn=jn,
                          c=dv.inc(): e.tensor_tensor(
                              out=interb(it, jn), in0=A, in1=up, op=ALU.mult)
                          .then_inc(c.sem, 1))
                        scr_free[sc] = dv.tok()
                        if grp == 7 and k == 1:
                            int_done = dv.tok()
                        gu_free[gb] = dv.tok()
                for hg in range(8):
                    db = hg % 2
                    firstd = tq4 == 0 and r == 0 and hg == 0
                    ew("sync",
                       ccw[2].tok() if firstd else None,
                       p3_dve if firstd else None,
                       x1_all if firstd else None,
                       xn1T_all if firstd else None,
                       dsl_free[db])
                    dt = dma("sync",
                             dslab(db).rearrange("p (jn r_) -> p jn r_",
                                                 r_=256),
                             dag_r[:, r, :, hg * 256:(hg + 1) * 256],
                             c_dsl[db])
                    dps = (pA, pB)[db]
                    ew("tensor", dt, int_done if hg == 0 else None,
                       dw_free[db])
                    for jn in range(16):
                        for k in range(2):
                            mm(dps[:, k * 512:(k + 1) * 512],
                               dslab(db)[:, jn * 256 + k * 128:
                                         jn * 256 + (k + 1) * 128],
                               interb(it, jn), jn == 0, jn == 15,
                               mark=(jn == 15 and k == 1))
                    dn_tok = pe.tok()
                    dsl_free[db] = dn_tok
                    if hg == 7:
                        int_free[it] = dn_tok
                    for k in range(2):
                        h = hg * 2 + k
                        ew("vector", dn_tok if k == 0 else None,
                           acc5_tok[h])
                        if r == 0:
                            E("vector", lambda e, h=h, k=k, dps=dps,
                              c=dv.inc(): e.tensor_copy(
                                  out=accF[h],
                                  in_=dps[:, k * 512:(k + 1) * 512])
                              .then_inc(c.sem, 1))
                        else:
                            E("vector", lambda e, h=h, k=k, dps=dps,
                              c=dv.inc(): e.tensor_tensor(
                                  out=accF[h], in0=accF[h],
                                  in1=dps[:, k * 512:(k + 1) * 512],
                                  op=ALU.add).then_inc(c.sem, 1))
                        acc5_tok[h] = dv.tok()
                        dw_free[db] = dv.tok()
                gu_free[0] = dv.tok()
            for tb in range(4):
                tglob = tq4 * 4 + tb
                ew("scalar", x15_free[tb % 2])
                x1l = dma("scalar", xtile[tb % 2],
                          x1_d[tglob * 128:(tglob + 1) * 128, :],
                          c_x[tb % 2])
                ew("tensor", acc5_tok[15], fin_ps_free,
                   gu_free[1] if tb == 0 else None)
                for h in range(16):
                    dst = pC if h < 8 else pD
                    off = (h % 8) * 128
                    tp(dst[:, off:off + 128],
                       accF[h][:, tb * 128:(tb + 1) * 128], sidf[:],
                       mark=(h == 15))
                ftr_tok = pe.tok()
                ew("vector", ftr_tok, x1l, out5_free[0])
                E("vector", lambda e, t=xtile[tb % 2]: e.tensor_tensor(
                    out=t[:, 0:1024], in0=t[:, 0:1024], in1=pC[:],
                    op=ALU.add))
                E("vector", lambda e, t=xtile[tb % 2], c=dv.inc():
                  e.tensor_tensor(
                      out=t[:, 1024:2048], in0=t[:, 1024:2048], in1=pD[:],
                      op=ALU.add).then_inc(c.sem, 1))
                fo_tok = dv.tok()
                fin_ps_free = fo_tok
                ew("scalar", fo_tok)
                out5_free[0] = dma("scalar",
                                   out_d[tglob * 128:(tglob + 1) * 128, :],
                                   xtile[tb % 2], c_dbg)
                x15_free[tb % 2] = out5_free[0]
            if tq4 < 3:
                final_tr = pe.tok()
                for h in range(16):
                    acc5_tok[h] = final_tr
                gu_free[1] = dv.tok()

        final_store = stq.tok()
        E("sync", lambda e, t=final_store: e.wait_ge(t[0], t[1]))

        with nc.Block() as block:
            @block.sync
            def _(eng):
                for g_, f in EV:
                    if g_ == "sync":
                        f(eng)

            @block.scalar
            def _(eng):
                for g_, f in EV:
                    if g_ == "scalar":
                        f(eng)

            @block.vector
            def _(eng):
                for g_, f in EV:
                    if g_ == "vector":
                        f(eng)

            @block.tensor
            def _(eng):
                for g_, f in EV:
                    if g_ == "tensor":
                        f(eng)

            @block.gpsimd
            def _(eng):
                for g_, f in EV:
                    if g_ == "gpsimd":
                        f(eng)

    return nc


_NC_CACHE = {}


def _get_nc(dbg=()):
    key = tuple(x[0] for x in dbg)
    if key not in _NC_CACHE:
        _NC_CACHE[key] = build(dbg)
    return _NC_CACHE[key]


def _consts():
    r4v = np.zeros((4, 128), dtype=np.float32)
    for b_ in range(4):
        r4v[b_, b_ * 32:(b_ + 1) * 32] = 0.25
    return dict(ident16=np.eye(128, dtype=_BF),
                identf=np.eye(128, dtype=np.float32),
                ones16=np.ones((128, 128), dtype=_BF),
                onesf=np.ones((128, 128), dtype=np.float32),
                r4=r4v)


def make_in_maps(x, wq_, wk_, wv_, wo_, gp, gs, up, us, dp, ds):
    consts = _consts()
    in_maps = []
    for c in range(NCORES):
        b, qs = divmod(c, 4)
        rows = slice(c * 2048, (c + 1) * 2048)
        in_maps.append(dict(
            x_in=np.ascontiguousarray(x[b, qs * 2048:(qs + 1) * 2048, :]),
            wq=wq_, wk=wk_, wv=wv_, wo=wo_,
            gpk=np.ascontiguousarray(gp[rows, :]),
            gsct=np.ascontiguousarray(gs[rows, :].T),
            upk=np.ascontiguousarray(up[rows, :]),
            usct=np.ascontiguousarray(us[rows, :].T),
            dpk=np.ascontiguousarray(dp[:, c * 1024:(c + 1) * 1024]),
            dsct=np.ascontiguousarray(ds[:, c * 64:(c + 1) * 64].T),
            **consts))
    return in_maps


def kernel(L0_attn_x, L0_attn_Wq, L0_attn_Wk, L0_attn_Wv, L0_attn_Wo,
           L0_ffn_gate_W_packed, L0_ffn_gate_scales, L0_ffn_up_W_packed,
           L0_ffn_up_scales, L0_ffn_down_W_packed, L0_ffn_down_scales):
    x = np.ascontiguousarray(np.asarray(L0_attn_x, dtype=np.float32))
    in_maps = make_in_maps(
        x,
        np.ascontiguousarray(np.asarray(L0_attn_Wq, dtype=np.float32)),
        np.ascontiguousarray(np.asarray(L0_attn_Wk, dtype=np.float32)),
        np.ascontiguousarray(np.asarray(L0_attn_Wv, dtype=np.float32)),
        np.ascontiguousarray(np.asarray(L0_attn_Wo, dtype=np.float32)),
        np.asarray(L0_ffn_gate_W_packed, dtype=np.int32),
        np.asarray(L0_ffn_gate_scales, dtype=np.float32),
        np.asarray(L0_ffn_up_W_packed, dtype=np.int32),
        np.asarray(L0_ffn_up_scales, dtype=np.float32),
        np.asarray(L0_ffn_down_W_packed, dtype=np.int32),
        np.asarray(L0_ffn_down_scales, dtype=np.float32))
    nc = _get_nc()
    res = run_bass_kernel_spmd(nc, in_maps, core_ids=list(range(NCORES)))
    out = np.empty((2, 8192, D), dtype=np.float32)
    for c in range(NCORES):
        b, qs = divmod(c, 4)
        out[b, qs * 2048:(qs + 1) * 2048, :] = np.asarray(
            res.results[c]["out"], dtype=np.float32)
    return out


# version 09
# speedup vs baseline: 1.1376x; 1.0811x over previous
import numpy as np
import ml_dtypes
from contextlib import ExitStack
import concourse.bass as bass
import concourse.mybir as mybir
from concourse.bass_utils import run_bass_kernel_spmd

F32 = mybir.dt.float32
BF16 = mybir.dt.bfloat16
I32 = mybir.dt.int32
AF = mybir.ActivationFunctionType
ALU = mybir.AluOpType
AX = mybir.AxisListType

D = 2048
NHEAD = 16
NKV = 4
HD = 128
KVD = 512
TLOC = 2048
NCORES = 8
RMS_EPS = float(np.finfo(np.float32).eps)
LN_EPS = 1e-5
V_EPS2 = 1e-24
ISCALE = float(HD) ** -0.5
GC1 = 0.7978845608028654 * 0.044715
GC0 = 0.7978845608028654

DVE_OF16 = 10

_BF = ml_dtypes.bfloat16


class Ch:
    __slots__ = ("sem", "n")

    def __init__(self, sem):
        self.sem = sem
        self.n = 0

    def inc(self, k=1):
        self.n += k
        return self

    def tok(self):
        return (self.sem, self.n)


def build(dbg=()):
    nc = bass.Bass("TRN2", target_bir_lowering=False, debug=False)

    x_in = nc.dram_tensor("x_in", [TLOC, D], F32, kind="ExternalInput").ap()
    wsrc = {
        "wq": nc.dram_tensor("wq", [D, D], F32, kind="ExternalInput").ap(),
        "wk": nc.dram_tensor("wk", [KVD, D], F32, kind="ExternalInput").ap(),
        "wv": nc.dram_tensor("wv", [KVD, D], F32, kind="ExternalInput").ap(),
        "wo": nc.dram_tensor("wo", [D, D], F32, kind="ExternalInput").ap(),
    }
    gpk = nc.dram_tensor("gpk", [2048, 1024], I32, kind="ExternalInput").ap()
    gsct = nc.dram_tensor("gsct", [64, 2048], F32, kind="ExternalInput").ap()
    upk = nc.dram_tensor("upk", [2048, 1024], I32, kind="ExternalInput").ap()
    usct = nc.dram_tensor("usct", [64, 2048], F32, kind="ExternalInput").ap()
    dpk = nc.dram_tensor("dpk", [2048, 1024], I32, kind="ExternalInput").ap()
    dsct = nc.dram_tensor("dsct", [64, 2048], F32, kind="ExternalInput").ap()
    ident16 = nc.dram_tensor("ident16", [128, 128], BF16, kind="ExternalInput").ap()
    identf = nc.dram_tensor("identf", [128, 128], F32, kind="ExternalInput").ap()
    ones16 = nc.dram_tensor("ones16", [128, 128], BF16, kind="ExternalInput").ap()
    onesf = nc.dram_tensor("onesf", [128, 128], F32, kind="ExternalInput").ap()
    r4 = nc.dram_tensor("r4", [4, 128], F32, kind="ExternalInput").ap()
    out_d = nc.dram_tensor("out", [TLOC, D], F32, kind="ExternalOutput").ap()

    wT_d = {
        "wq": nc.dram_tensor("wqT_d", [D, D], BF16).ap(),
        "wk": nc.dram_tensor("wkT_d", [D, KVD], BF16).ap(),
        "wv": nc.dram_tensor("wvT_d", [D, KVD], BF16).ap(),
        "wo": nc.dram_tensor("woT_d", [D, D], BF16).ap(),
    }
    sct_d = [gsct, usct, dsct]
    kag_i = nc.dram_tensor("kag_i", [KVD, TLOC], BF16)
    kag_o = nc.dram_tensor("kag_o", [4 * KVD, TLOC], BF16)
    vag_i = nc.dram_tensor("vag_i", [TLOC, KVD], BF16)
    vag_o = nc.dram_tensor("vag_o", [4 * TLOC, KVD], BF16)
    gag_i = nc.dram_tensor("gag_i", [D, 2048], BF16)
    gag_o = nc.dram_tensor("gag_o", [NCORES * D, 2048], BF16)
    uag_i = nc.dram_tensor("uag_i", [D, 2048], BF16)
    uag_o = nc.dram_tensor("uag_o", [NCORES * D, 2048], BF16)
    dag_i = nc.dram_tensor("dag_i", [2048, D], BF16)
    dag_o = nc.dram_tensor("dag_o", [NCORES * 2048, D], BF16)
    x1_d = nc.dram_tensor("x1_d", [TLOC, D], F32).ap()

    dbg_d = {}
    for name, shape, dt in dbg:
        dbg_d[name] = nc.dram_tensor("dbg_" + name, shape, dt,
                                     kind="ExternalOutput").ap()

    EV = []

    def E(engine, fn):
        EV.append((engine, fn))

    def ew(engine, *toks):
        toks = [t for t in toks if t is not None and t[1] > 0]
        if toks:
            def f(e, toks=tuple(toks)):
                for s, v in toks:
                    e.wait_ge(s, v)
            E(engine, f)

    with ExitStack() as ctx:
        def sb(name, shape, dtype):
            return ctx.enter_context(nc.sbuf_tensor(name, shape, dtype))

        def psum(name, shape, dtype):
            return ctx.enter_context(nc.psum_tensor(name, shape, dtype))

        def sem(name):
            return Ch(ctx.enter_context(nc.semaphore(name)))

        arena_a = sb("arena_a", [128, 32768], BF16)
        arena_b = sb("arena_b", [128, 32768], BF16)
        arena_c = sb("arena_c", [128, 8192], F32)
        vnT = sb("vnT", [128, 8192], BF16)
        stag = sb("stag", [128, 4096], F32)
        wogrn = sb("wogrn", [128, 4096], BF16)
        kstg = sb("kstg", [128, 512], BF16)
        stats = sb("stats", [128, 16], F32)
        epsb = sb("epsb", [128, 4], F32)
        sid16 = sb("sid16", [128, 128], BF16)
        sidf = sb("sidf", [128, 128], F32)
        son16 = sb("son16", [128, 128], BF16)
        sonf = sb("sonf", [128, 128], F32)
        sr4 = sb("sr4", [4, 128], F32)
        sct_t = sb("sct_t", [4, 1024], F32)

        wrow16 = arena_b[:, 16384:18432]
        wstg16 = arena_b[:, 18432:20480]
        vts = wstg16
        vstg = kstg
        sq5 = wogrn.bitcast(F32)[:, 0:1024]
        st5 = wogrn.bitcast(F32)[:, 1024:2048]
        xn16_p1 = wogrn[:, 0:2048]
        xn16_p4 = vnT[:, 4096:6144]
        p4st = vnT.bitcast(F32)

        cF = arena_c
        t_e = cF[:, 0:2048].bitcast(I32)
        t_s2 = cF[:, 2048:4096].bitcast(I32)
        t_a2 = cF[:, 4096:6144]
        spk_v = cF[:, 6144:7168].bitcast(I32)
        vals_v = cF[:, 6144:7168].bitcast(BF16)
        svalT = cF[:, 7168:7424].bitcast(BF16)
        p1st = cF[:, 7424:7808]
        srstd = stag[:, 0:1024]
        sqrms = stag[:, 1024:1280].bitcast(BF16)
        acc_d = cF[:, 0:1024]
        acc_p = cF[:, 1024:2048]
        sp = [cF[:, 2048:2560].bitcast(BF16), cF[:, 2560:3072].bitcast(BF16),
              cF[:, 6656:7168].bitcast(BF16), cF[:, 7168:7680].bitcast(BF16)]
        prodb = cF[:, 3072:3584].bitcast(BF16)
        tmpc = cF[:, 3584:4608]
        recip = cF[:, 4608:5632]
        ytmp = cF[:, 5632:6656]
        outT = [cF[:, j * 512:(j + 1) * 512] for j in range(16)]
        accF = outT

        def xnT(jc):
            return arena_a[:, jc * 2048:(jc + 1) * 2048]
        xn1T = xnT

        def ktb(buf):
            return arena_a[:, buf * 8192:(buf + 1) * 8192]

        def vvb(buf):
            return arena_a[:, 16384 + buf * 8192:16384 + (buf + 1) * 8192]

        def qyb(h):
            return arena_b[:, h * 2048:(h + 1) * 2048]

        def interb(buf, jn):
            return arena_b[:, buf * 8192 + jn * 512:buf * 8192 + (jn + 1) * 512]

        def gslab(buf):
            return arena_b[:, 16384 + buf * 4096:16384 + (buf + 1) * 4096]

        def uslab(buf):
            return arena_b[:, 24576 + buf * 4096:24576 + (buf + 1) * 4096]

        def dslab(buf):
            return vnT[:, buf * 4096:(buf + 1) * 4096]

        pA = psum("pA", [128, 1024], F32)
        pB = psum("pB", [128, 1024], F32)
        pC = psum("pC", [128, 1024], F32)
        pD = psum("pD", [128, 1024], F32)
        pA16 = pA.bitcast(BF16)
        pB16 = pB.bitcast(BF16)
        pC16 = pC.bitcast(BF16)
        pD16 = pD.bitcast(BF16)
        accslot = [pA[:, 0:512], pA[:, 512:1024], pB[:, 0:512], pB[:, 512:1024]]

        ldq = sem("ldq")
        ldx = sem("ldx")
        ldp = sem("ldp")
        stq = sem("stq")
        c_pk = sem("c_pk")
        c_sct = [sem("c_sct0"), sem("c_sct1")]
        c_wrow = sem("c_wrow")
        c_awst = sem("c_awst")
        c_grn = [sem("c_grn0"), sem("c_grn1")]
        c_x = [sem("c_x0"), sem("c_x1")]
        c_kld = [sem("c_kld0"), sem("c_kld1")]
        c_vld = [sem("c_vld0"), sem("c_vld1")]
        c_kst = sem("c_kst")
        c_vst = sem("c_vst")
        c_gsl = [sem("c_gsl0"), sem("c_gsl1")]
        c_usl = [sem("c_usl0"), sem("c_usl1")]
        c_dsl = [sem("c_dsl0"), sem("c_dsl1")]
        c_x1 = sem("c_x1s")
        c_dbg = sem("c_dbg")
        pe = sem("pe")
        dv = sem("dv")
        ac = sem("ac")
        pl = sem("pl")
        cck = sem("cck")
        ccv = sem("ccv")
        ccw = [sem("ccg"), sem("ccu"), sem("ccd")]

        def dma(queue, dst, src, ch):
            E(queue, lambda e, d=dst, s=src, c=ch.inc(16): e.dma_start(
                out=d, in_=s).then_inc(c.sem, 16))
            return ch.tok()

        wT_r = {k: v.rearrange("(jc p) o -> p jc o", p=128)
                for k, v in wT_d.items()}

        def mm(acc_ap, lhsT_ap, rhs_ap, start, stop, mark=False):
            if mark:
                c = pe.inc()
                E("tensor", lambda e, a=acc_ap, l=lhsT_ap, r=rhs_ap, s1=start,
                  s2=stop, c=c: e.matmul(a, l, r, start=s1, stop=s2)
                  .then_inc(c.sem, 1))
            else:
                E("tensor", lambda e, a=acc_ap, l=lhsT_ap, r=rhs_ap, s1=start,
                  s2=stop: e.matmul(a, l, r, start=s1, stop=s2))

        def tp(dst_ap, src_ap, ident, mark=False):
            if mark:
                c = pe.inc()
                E("tensor", lambda e, d=dst_ap, s=src_ap, i=ident, c=c:
                  e.transpose(d, s, i).then_inc(c.sem, 1))
            else:
                E("tensor", lambda e, d=dst_ap, s=src_ap, i=ident:
                  e.transpose(d, s, i))

        deq_state = dict(spk_free=None, pd_free=None, sval_free=None,
                         wrow_free=None)
        shard_toks = [None, None, None]
        deq_queue = [(w, nt) for w in range(3) for nt in range(16)]
        pk_srcs = [gpk, upk, dpk]
        ag_is = [gag_i, uag_i, dag_i]

        def emit_decode_tile(job):
            wi, nt = job
            s = deq_state
            dmp_t = None
            ew("sync", s["spk_free"])
            pk_tok = dma("sync", spk_v[:],
                         pk_srcs[wi][nt * 128:(nt + 1) * 128, :], c_pk)
            if wi == 0 and nt == 0 and "dpk" in dbg_d:
                ew("scalar", pk_tok)
                pkd = dma("scalar", dbg_d["dpk"].bitcast(I32), spk_v[:], c_dbg)
                ew("vector", pkd)
            ew("vector", pk_tok, s["wrow_free"], consts_tok)
            def dvstep(fn):
                c = dv.inc()
                E("vector", lambda e, fn=fn, c=c: fn(e).then_inc(c.sem, 1))
                ew("vector", dv.tok())
            i2 = lambda ap: ap.rearrange("p (c two) -> p c two", two=2)
            dvstep(lambda e: e.tensor_scalar(
                out=i2(t_s2)[:, :, 0], in0=spk_v[:], scalar1=8, scalar2=28,
                op0=ALU.bitwise_and, op1=ALU.logical_shift_left))
            dvstep(lambda e: e.tensor_scalar(
                out=i2(t_s2)[:, :, 1], in0=spk_v[:], scalar1=0x80, scalar2=24,
                op0=ALU.bitwise_and, op1=ALU.logical_shift_left))
            dvstep(lambda e: e.tensor_scalar(
                out=i2(t_e)[:, :, 0], in0=spk_v[:], scalar1=7, scalar2=22,
                op0=ALU.bitwise_and, op1=ALU.logical_shift_left))
            dvstep(lambda e: e.tensor_scalar(
                out=i2(t_e)[:, :, 1], in0=spk_v[:], scalar1=0x70, scalar2=18,
                op0=ALU.bitwise_and, op1=ALU.logical_shift_left))
            dvstep(lambda e: e.tensor_scalar(
                out=t_e[:], in0=t_e[:], scalar1=0x40000000, scalar2=None,
                op0=ALU.add))
            dvstep(lambda e: e.tensor_scalar(
                out=t_a2[:], in0=t_e[:].bitcast(F32), scalar1=2.0,
                scalar2=-4.0, op0=ALU.mult, op1=ALU.add))
            dvstep(lambda e: e.tensor_tensor(
                out=t_a2[:], in0=t_e[:].bitcast(F32), in1=t_a2[:],
                op=ALU.min))
            dvstep(lambda e: e.tensor_scalar(
                out=t_s2[:], in0=t_s2[:], scalar1=0x3F800000, scalar2=None,
                op0=ALU.add))
            dvstep(lambda e: e.tensor_tensor(
                out=vals_v[:], in0=t_a2[:], in1=t_s2[:].bitcast(F32),
                op=ALU.mult))
            dec_tok = dv.tok()
            if wi == 0 and nt == 1 and "dvals1" in dbg_d:
                ew("scalar", dec_tok)
                s["spk_free"] = dma("scalar", dbg_d["dvals1"], vals_v[:], c_dbg)
            if wi == 0 and nt == 0 and "dte" in dbg_d:
                ew("scalar", dec_tok)
                ted = dma("scalar", dbg_d["dte"].bitcast(I32), t_e[:], c_dbg)
                tad = dma("scalar", dbg_d["dta"], t_a2[:], c_dbg)
                tsd = dma("scalar", dbg_d["dts"].bitcast(I32), t_s2[:], c_dbg)
                ew("tensor", tsd)
            if wi == 0 and nt == 0 and "dvals" in dbg_d:
                ew("scalar", dec_tok)
                s["spk_free"] = dma("scalar", dbg_d["dvals"], vals_v[:], c_dbg)
            for g in range(4):
                sb_ = g % 2
                ew("sync", s.get("sct_free%d" % sb_))
                sct_tok = dma(
                    "sync", sct_t[:, sb_ * 512:(sb_ + 1) * 512],
                    sct_d[wi].rearrange("(jc p) n -> p jc n", p=4)
                    [:, 4 * g:4 * g + 4, nt * 128:(nt + 1) * 128], c_sct[sb_])
                ew("tensor", dec_tok if g == 0 else None, sct_tok,
                   s["pd_free"], s["sval_free"])
                for k in range(4):
                    jc = g * 4 + k
                    tp(pD16[:, k * 128:(k + 1) * 128],
                       vals_v[:, jc * 128:(jc + 1) * 128], sid16[:])
                    mm(pD[:, 512 + k * 128:512 + (k + 1) * 128], sr4[:],
                       sct_t[:, sb_ * 512 + k * 128:sb_ * 512 + (k + 1) * 128],
                       True, True, mark=(k == 3))
                s["sct_free%d" % sb_] = (pe.sem, pe.n)
                grp_tok = pe.tok()
                ew("scalar", grp_tok, s["wrow_free"] if g == 0 else None)
                E("scalar", lambda e, c=ac.inc(): e.copy(
                    out=svalT[:], in_=pD16[:, 0:512]).then_inc(c.sem, 1))
                evt = ac.tok()
                ew("vector", evt)
                E("vector", lambda e, g=g, c=dv.inc(): e.tensor_tensor(
                    out=wrow16[:, g * 512:(g + 1) * 512], in0=svalT[:],
                    in1=pD[:, 512:1024], op=ALU.mult).then_inc(c.sem, 1))
                s["sval_free"] = dv.tok()
                s["pd_free"] = dv.tok()
            if not (wi == 0 and nt == 0 and "dvals" in dbg_d):
                s["spk_free"] = pe.tok()
            wrow_tok = dv.tok()
            if wi == 0 and nt == 1 and "dwrow1" in dbg_d:
                ew("scalar", wrow_tok)
                dmp_t = dma("scalar", dbg_d["dwrow1"], wrow16[:], c_dbg)
                s["spk_free"] = dmp_t
            if wi == 0 and nt == 0 and "dwrow" in dbg_d:
                ew("scalar", wrow_tok)
                dmp_t = dma("scalar", dbg_d["dwrow"], wrow16[:], c_dbg)
            ew("sync", wrow_tok,
               dmp_t if (wi == 0 and nt == 0 and "dwrow" in dbg_d) else None)
            s["wrow_free"] = dma(
                "sync",
                ag_is[wi].ap().rearrange("(jc p) n -> p jc n", p=128)
                [:, :, nt * 128:(nt + 1) * 128],
                wrow16[:].rearrange("p (jc r) -> p jc r", r=128), c_wrow)
            if nt == 15:
                shard_toks[wi] = c_wrow.tok()

        def pump(n=1):
            for _ in range(n):
                if deq_queue:
                    emit_decode_tile(deq_queue.pop(0))

        for dst, src in ((sid16, ident16), (sidf, identf), (son16, ones16),
                         (sonf, onesf), (sr4, r4)):
            dma("scalar", dst[:], src, ldx)
        consts_tok = ldx.tok()
        E("vector", lambda e: e.memset(epsb[:, 0:1], LN_EPS))
        E("vector", lambda e: e.memset(epsb[:, 1:2], RMS_EPS))
        E("vector", lambda e: e.memset(epsb[:, 2:3], V_EPS2))

        xtile = [stag[:, 0:2048], stag[:, 2048:4096]]
        xload_free = [None, None]

        ln_state = dict(xn_free=None)

        def ln_chain2(src_tile, ln_loaded, xb, tglob, pm, pq, sA, sB, sC,
                      tr_wait, ps_wait):
            dst3 = arena_a[:].rearrange("p (jc r) -> p jc r", r=2048)[
                :, :, tglob * 128:(tglob + 1) * 128]
            ew("vector", ln_loaded, consts_tok, ln_state["xn_free"])
            E("vector", lambda e, t=src_tile, xb=xb, c=dv.inc():
              e.tensor_copy(out=xb, in_=t).then_inc(c.sem, 1))
            cast_tok = dv.tok()
            ew("tensor", cast_tok, *tr_wait)
            for jc in range(16):
                tp(pA16[:, jc * 128:(jc + 1) * 128],
                   xb[:, jc * 128:(jc + 1) * 128], sid16[:], mark=(jc == 15))
            tr_tok = pe.tok()
            ew("scalar", tr_tok)
            E("scalar", lambda e, d=dst3, c=ac.inc(): e.copy(
                out=d, in_=pA16[:, 0:2048].rearrange(
                    "p (jc r) -> p jc r", r=128)).then_inc(c.sem, 1))
            evac_tok = ac.tok()
            ew("scalar", evac_tok)
            E("scalar", lambda e, d=dst3, xb=xb, c=ac.inc(): e.activation(
                out=xb, in_=d, func=AF.Square).then_inc(c.sem, 1))
            sqa_tok = ac.tok()
            ew("tensor", sqa_tok, *ps_wait)
            for jc in range(16):
                mm(pm, son16[:],
                   arena_a[:, jc * 2048 + tglob * 128:
                           jc * 2048 + tglob * 128 + 128],
                   jc == 0, jc == 15)
            for jc in range(16):
                mm(pq, son16[:], xb[:, jc * 128:(jc + 1) * 128],
                   jc == 0, jc == 15, mark=(jc == 15))
            smm_tok = pe.tok()
            ew("vector", smm_tok)
            E("vector", lambda e, pm=pm, sA=sA: e.tensor_scalar(
                out=sA, in0=pm, scalar1=1.0 / D, scalar2=None, op0=ALU.mult))
            E("vector", lambda e, sA=sA, sB=sB: e.tensor_tensor(
                out=sB, in0=sA, in1=sA, op=ALU.mult))
            E("vector", lambda e, pq=pq, sB=sB, c=dv.inc():
              e.scalar_tensor_tensor(
                  out=sB, in0=pq, scalar=1.0 / D, in1=sB,
                  op0=ALU.mult, op1=ALU.subtract).then_inc(c.sem, 1))
            var_tok = dv.tok()
            ew("scalar", var_tok)
            E("scalar", lambda e, sB=sB, sC=sC, c=ac.inc(): e.activation(
                out=sC, in_=sB, func=AF.Ln, bias=epsb[:, 0:1], scale=1.0)
              .then_inc(c.sem, 1))
            ln_tok = ac.tok()
            ew("scalar", ln_tok)
            E("scalar", lambda e, sC=sC, c=ac.inc(): e.activation(
                out=sC, in_=sC, func=AF.Exp, scale=-0.5).then_inc(c.sem, 1))
            exp_tok = ac.tok()
            ew("vector", exp_tok)
            E("vector", lambda e, d=dst3, sA=sA: e.tensor_tensor(
                out=d, in0=d, in1=sA.unsqueeze(1).broadcast_to([128, 16, 128]),
                op=ALU.subtract))
            E("vector", lambda e, d=dst3, sC=sC, c=dv.inc(): e.tensor_tensor(
                out=d, in0=d, in1=sC.unsqueeze(1).broadcast_to([128, 16, 128]),
                op=ALU.mult).then_inc(c.sem, 1))
            done_tok = dv.tok()
            ln_state["xn_free"] = smm_tok
            return done_tok, evac_tok, smm_tok, var_tok

        pa_free = None
        ps_free = None
        for tt in range(16):
            ew("scalar", xload_free[tt % 2])
            x_tok = dma("scalar", xtile[tt % 2],
                        x_in[tt * 128:(tt + 1) * 128, :], c_x[tt % 2])
            xn_tok, ev_t, smm_t, st_t = ln_chain2(
                xtile[tt % 2], x_tok, xn16_p1, tt,
                pC[:, 0:128], pC[:, 512:640],
                p1st[:, 0:128], p1st[:, 128:256], p1st[:, 256:384],
                tr_wait=(pa_free,), ps_wait=(ps_free,))
            xload_free[tt % 2] = xn_tok
            pa_free = ev_t
            ps_free = st_t
        xnT_all = dv.tok()
        if "xn" in dbg_d:
            ew("scalar", xnT_all)
            dma("scalar", dbg_d["xn"], arena_a[:], c_dbg)

        aw_state = dict(pb_free=None)
        aw_stage = [wstg16, cF[:, 0:1024].bitcast(BF16)]
        aw_stage_q = [cF[:, 0:1024].bitcast(BF16), cF[:, 1024:2048].bitcast(BF16)]
        aw_wst_free = [None, None]
        aw_cast_free = [None, None]
        wT_written = {}
        wT_tile_tok = {}

        def aw_prep(wname, ntile, first=None, dbuf=False):
            toks = []
            stages = aw_stage_q if dbuf else aw_stage
            for ot in range(ntile):
                sl = (ot % 2) if dbuf else 0
                stg = stages[sl]
                ew("gpsimd", first if ot == 0 else None,
                   aw_wst_free[sl], aw_cast_free[sl])
                wl_tok = dma("gpsimd", stg[:],
                             wsrc[wname][ot * 128:(ot + 1) * 128, :], ldp)
                ew("tensor", wl_tok, aw_state["pb_free"])
                for jc in range(16):
                    tp(pB16[:, jc * 128:(jc + 1) * 128],
                       stg[:, jc * 128:(jc + 1) * 128], sid16[:],
                       mark=(jc == 15))
                tr_tok = pe.tok()
                ew("vector", tr_tok)
                E("vector", lambda e, stg=stg, c=dv.inc(): e.tensor_copy(
                    out=stg[:], in_=pB16[:, 0:2048]).then_inc(c.sem, 1))
                ev_tok = dv.tok()
                aw_state["pb_free"] = ev_tok
                aw_cast_free[sl] = ev_tok
                ew("sync", ev_tok)
                aw_wst_free[sl] = dma(
                    "sync", wT_r[wname][:, :, ot * 128:(ot + 1) * 128],
                    stg[:].rearrange("p (jc r) -> p jc r", r=128), c_awst)
                toks.append(c_awst.tok())
                pump(1)
            wT_written[wname] = c_awst.tok()
            wT_tile_tok[wname] = toks
            return toks

        aw_prep("wk", 4)
        aw_evac_all = dv.tok()

        p2 = dict(slot=0, acc_free=[None] * 4, ms_free=ps_free)
        vts_done = [None] * 4
        vts_state = dict(free=None)
        kst_free = [None]
        vst_free = [None]
        pcV_free = [None]
        qproj_done = [None] * NHEAD
        grn_free = [None, None]

        def proj_tile(stat_fn, tq, kind, dest_jo, extra=()):
            sl = p2["slot"] % 4
            p2["slot"] += 1
            acc = accslot[sl]
            ew("tensor", *extra, p2["acc_free"][sl])
            for jc in range(16):
                mm(acc, stat_fn(jc), xnT(jc)[:, tq * 512:(tq + 1) * 512],
                   jc == 0, jc == 15, mark=(jc == 15))
            mm_tok = pe.tok()
            ew("scalar", mm_tok)
            E("scalar", lambda e, acc=acc, c=ac.inc(): e.activation(
                out=sqrms[:], in_=acc, func=AF.Square).then_inc(c.sem, 1))
            sq_tok = ac.tok()
            ew("tensor", sq_tok, p2["ms_free"])
            mm(pC[:, 0:512], son16[:], sqrms[:], True, True, mark=True)
            ms_tok = pe.tok()
            ew("scalar", ms_tok)
            if kind in ("q", "k"):
                E("scalar", lambda e: e.activation(
                    out=srstd[:, 0:512], in_=pC[:, 0:512], func=AF.Ln,
                    bias=epsb[:, 1:2], scale=1.0 / 128))
            else:
                E("scalar", lambda e: e.activation(
                    out=srstd[:, 0:512], in_=pC[:, 0:512], func=AF.Ln,
                    bias=epsb[:, 2:3], scale=1.0))
            E("scalar", lambda e, c=ac.inc(): e.activation(
                out=srstd[:, 512:1024], in_=srstd[:, 0:512], func=AF.Exp,
                scale=-0.5).then_inc(c.sem, 1))
            rstd_tok = ac.tok()
            p2["ms_free"] = rstd_tok
            if kind == "q":
                ew("vector", rstd_tok,
                   shard_toks[2] if (dest_jo == 8 and tq == 0
                                     and shard_toks[2]) else None)
                E("vector", lambda e, jo=dest_jo, tq=tq, acc=acc, c=dv.inc():
                  e.tensor_tensor(
                      out=qyb(jo)[:, tq * 512:(tq + 1) * 512], in0=acc,
                      in1=srstd[:, 512:1024], op=ALU.mult).then_inc(c.sem, 1))
                p2["acc_free"][sl] = dv.tok()
                qproj_done[dest_jo] = dv.tok()
            elif kind == "k":
                ew("vector", rstd_tok, kst_free[0])
                E("vector", lambda e, acc=acc, c=dv.inc(): e.tensor_tensor(
                    out=kstg[:], in0=acc, in1=srstd[:, 512:1024],
                    op=ALU.mult).then_inc(c.sem, 1))
                k_tok = dv.tok()
                p2["acc_free"][sl] = k_tok
                ew("scalar", k_tok)
                kst_free[0] = dma(
                    "scalar",
                    kag_i.ap()[dest_jo * 128:(dest_jo + 1) * 128,
                               tq * 512:(tq + 1) * 512], kstg[:], c_kst)
            else:
                ew("vector", rstd_tok, vts_state["free"],
                   wT_written["wv"] if vts_state["free"] is None else None)
                E("vector", lambda e, jo=dest_jo, tq=tq, acc=acc:
                  e.tensor_tensor(
                      out=vnT[:, jo * 2048 + tq * 512:
                              jo * 2048 + (tq + 1) * 512],
                      in0=acc, in1=srstd[:, 512:1024], op=ALU.mult))
                E("vector", lambda e, jo=dest_jo, acc=acc, c=dv.inc():
                  e.tensor_copy(
                      out=vts[:, jo * 512:(jo + 1) * 512], in_=acc)
                  .then_inc(c.sem, 1))
                p2["acc_free"][sl] = dv.tok()
                vts_done[dest_jo] = dv.tok()

        def load_grn(wname, jo, extra=()):
            gb = jo % 2
            ew("scalar", *extra, wT_tile_tok[wname][jo], grn_free[gb])
            return gb, dma(
                "scalar",
                wogrn[:, gb * 2048:(gb + 1) * 2048].rearrange(
                    "p (jc r) -> p jc r", r=128),
                wT_r[wname][:, :, jo * 128:(jo + 1) * 128], c_grn[gb])

        for jo in range(4):
            gb, g_tok = load_grn(
                "wk", jo, extra=(xnT_all,) if jo == 0 else ())
            for tq in range(4):
                proj_tile(
                    lambda jc, gb=gb: wogrn[:, gb * 2048 + jc * 128:
                                            gb * 2048 + (jc + 1) * 128],
                    tq, "k", jo,
                    extra=((g_tok, xnT_all, aw_evac_all)
                           if (jo == 0 and tq == 0) else
                           (g_tok,) if tq == 0 else ()))
                pump(2)
            grn_free[gb] = pe.tok()

        aw_prep("wv", 4)
        for tq in range(4):
            for jo in range(4):
                gb, g_tok = load_grn("wv", jo)
                proj_tile(
                    lambda jc, gb=gb: wogrn[:, gb * 2048 + jc * 128:
                                            gb * 2048 + (jc + 1) * 128],
                    tq, "v", jo, extra=(g_tok,))
                grn_free[gb] = pe.tok()
            for tc in range(4):
                tglob = tq * 4 + tc
                ew("tensor", vts_done[3], pcV_free[0])
                for jo in range(4):
                    tp(pC16[:, 1024 + jo * 128:1024 + (jo + 1) * 128],
                       vts[:, jo * 512 + tc * 128:jo * 512 + (tc + 1) * 128],
                       sid16[:], mark=(jo == 3))
                vt_tok = pe.tok()
                ew("scalar", vt_tok, vst_free[0],
                   kst_free[0] if tglob == 0 else None)
                E("scalar", lambda e, c=ac.inc(): e.copy(
                    out=vstg[:], in_=pC16[:, 1024:1536]).then_inc(c.sem, 1))
                ev_tok = ac.tok()
                pcV_free[0] = ev_tok
                ew("scalar", ev_tok)
                vst_free[0] = dma(
                    "scalar", vag_i.ap()[tglob * 128:(tglob + 1) * 128, :],
                    vstg[:], c_vst)
            vts_state["free"] = pe.tok()
            pump(2)
        kag_done = kst_free[0]
        vag_done = vst_free[0]
        v_mm_done = pe.tok()

        GROUPS = [[0, 1, 2, 3], [4, 5, 6, 7]]
        ew("gpsimd", kag_done)
        E("gpsimd", lambda e, c=cck.inc(): e.collective_compute(
            "AllGather", ALU.bypass, replica_groups=GROUPS,
            ins=[kag_i.ap().opt()], outs=[kag_o.ap().opt()]).then_inc(c.sem, 1))
        ew("gpsimd", vag_done)
        E("gpsimd", lambda e, c=ccv.inc(): e.collective_compute(
            "AllGather", ALU.bypass, replica_groups=GROUPS,
            ins=[vag_i.ap().opt()], outs=[vag_o.ap().opt()]).then_inc(c.sem, 1))

        ew("gpsimd", pe.tok(), dv.tok())
        aw_prep("wq", 16, dbuf=True)
        for jo in range(16):
            gb, g_tok = load_grn(
                "wq", jo, extra=((v_mm_done,) if jo == 0 else ()))
            for tq in range(4):
                proj_tile(
                    lambda jc, gb=gb: wogrn[:, gb * 2048 + jc * 128:
                                            gb * 2048 + (jc + 1) * 128],
                    tq, "q", jo, extra=(g_tok,) if tq == 0 else ())
                pump(1)
            grn_free[gb] = pe.tok()
        qproj_all = dv.tok()
        aw_prep("wo", 16, dbuf=True)
        p2_pe_done = pe.tok()
        pump(64)

        for wi in range(3):
            ew("gpsimd", shard_toks[wi])
            ag_o2 = [gag_o, uag_o, dag_o][wi]
            ag_i2 = ag_is[wi]
            E("gpsimd", lambda e, ag_i2=ag_i2, ag_o2=ag_o2, c=ccw[wi].inc():
              e.collective_compute(
                  "AllGather", ALU.bypass,
                  replica_groups=[list(range(NCORES))],
                  ins=[ag_i2.ap().opt()], outs=[ag_o2.ap().opt()])
              .then_inc(c.sem, 1))

        if "qy" in dbg_d:
            ew("scalar", qproj_all)
            dma("scalar", dbg_d["qy"], arena_b[:, 0:32768], c_dbg)
        if "vn" in dbg_d:
            ew("scalar", qproj_all)
            dma("scalar", dbg_d["vn"], vnT[:], c_dbg)
        if "gag" in dbg_d:
            ew("scalar", ccw[0].tok())
            dma("scalar", dbg_d["gag"], gag_o.ap(), stq)
        if "dag" in dbg_d:
            ew("scalar", ccw[2].tok())
            dma("scalar", dbg_d["dag"], dag_o.ap(), stq)
        if "kag" in dbg_d:
            ew("scalar", cck.tok())
            dma("scalar", dbg_d["kag"], kag_o.ap(), stq)
        if "vag" in dbg_d:
            ew("scalar", ccv.tok())
            dma("scalar", dbg_d["vag"], vag_o.ap(), stq)

        pS2 = [pA, pB]
        kld = [None, None]
        vld = [None, None]
        kuse = [None, None]
        vuse = [None, None]
        pS_free = [None, None]
        sp_free = [None] * 4
        pB_drain = [None, None]
        exp_tok_q = [None] * 128
        pC_drain = [None, None]
        pD_drain = [None, None]
        y_done = [[None] * 4 for _ in range(NHEAD)]

        for g in range(NKV):
            kb = g % 2
            ew("scalar", cck.tok() if g == 0 else None,
               p2_pe_done if g == 0 else None, kuse[kb])
            for rb in range(4):
                kld[kb] = dma(
                    "scalar", ktb(kb)[:, rb * 2048:(rb + 1) * 2048],
                    kag_o.ap()[rb * KVD + g * 128:rb * KVD + (g + 1) * 128, :],
                    c_kld[kb])
            ew("scalar", ccv.tok() if g == 0 else None, vuse[kb])
            vld[kb] = dma(
                "scalar", vvb(kb)[:],
                vag_o.ap().rearrange("(j p) d -> p j d", p=128)
                [:, :, g * 128:(g + 1) * 128], c_vld[kb])
            for hp in range(2):
                hA = 4 * g + 2 * hp
                for tq in range(4):
                    qsl = slice(tq * 512, (tq + 1) * 512)
                    for j in range(65):
                        if j < 64:
                            for hi in range(2):
                                h = hA + hi
                                ew("tensor",
                                   kld[kb] if (hp == 0 and tq == 0 and j == 0
                                               and hi == 0) else None,
                                   qproj_done[h] if j == 0 else None,
                                   qproj_all if (g == 0 and hp == 0 and tq == 0
                                                 and j == 0 and hi == 0)
                                   else None,
                                   pS_free[j % 2] if hi == 0 else None)
                                mm(pS2[j % 2][:, hi * 512:(hi + 1) * 512],
                                   ktb(kb)[:, j * 128:(j + 1) * 128],
                                   qyb(h)[:, qsl], True, True, mark=(hi == 1))
                            sc_tok = pe.tok()
                            kuse[kb] = sc_tok
                            sps = j % 4
                            ew("scalar", sc_tok, sp_free[sps])
                            E("scalar", lambda e, j=j, sps=sps, c=ac.inc():
                              e.activation(out=sp[sps][:], in_=pS2[j % 2][:],
                                           func=AF.Exp, scale=ISCALE)
                              .then_inc(c.sem, 1))
                            pS_free[j % 2] = ac.tok()
                            exp_tok_q[j] = ac.tok()
                        for hi in range(2):
                            if j > 0:
                                jj = j - 1
                                sps = jj % 4
                                spv = sp[sps][:, hi * 512:(hi + 1) * 512]
                                ew("tensor", exp_tok_q[jj] if hi == 0 else None,
                                   vld[kb] if (hp == 0 and tq == 0 and jj == 0
                                               and hi == 0) else None,
                                   pC_drain[hi] if jj == 0 else None,
                                   pD_drain[hi] if jj == 0 else None)
                                mm(pC[:, hi * 512:(hi + 1) * 512],
                                   vvb(kb)[:, jj * 128:(jj + 1) * 128],
                                   spv, jj == 0, jj == 63)
                                mm(pD[:, hi * 512:(hi + 1) * 512], son16[:],
                                   spv, jj == 0, jj == 63, mark=(hi == 1))
                                pv_tok = pe.tok()
                                if hi == 1:
                                    sp_free[sps] = pv_tok
                                vuse[kb] = pv_tok
                    for hi in range(2):
                        h = hA + hi
                        hs = slice(hi * 512, (hi + 1) * 512)
                        ew("scalar", pe.tok())
                        E("scalar", lambda e, hs=hs, c=ac.inc():
                          e.activation(out=recip[:, hs], in_=pD[:, hs],
                                       func=AF.Ln, scale=1.0)
                          .then_inc(c.sem, 1))
                        ln_done = ac.tok()
                        ew("scalar", ac.tok())
                        E("scalar", lambda e, hs=hs, c=ac.inc():
                          e.activation(out=recip[:, hs], in_=recip[:, hs],
                                       func=AF.Exp, scale=-1.0)
                          .then_inc(c.sem, 1))
                        rc_tok = ac.tok()
                        ew("vector", rc_tok)
                        E("vector", lambda e, hs=hs, c=dv.inc():
                          e.tensor_tensor(
                              out=ytmp[:, hs], in0=pC[:, hs],
                              in1=recip[:, hs], op=ALU.mult)
                          .then_inc(c.sem, 1))
                        pC_drain[hi] = dv.tok()
                        E("vector", lambda e, g=g, hs=hs, qsl=qsl, c=dv.inc():
                          e.tensor_tensor(
                              out=prodb[:, hs], in0=ytmp[:, hs],
                              in1=vnT[:, g * 2048:(g + 1) * 2048][:, qsl],
                              op=ALU.mult).then_inc(c.sem, 1))
                        pr_tok = dv.tok()
                        ew("tensor", pr_tok, ln_done)
                        mm(pD[:, hs], son16[:], prodb[:, hs], True, True,
                           mark=True)
                        sr_tok = pe.tok()
                        ew("vector", sr_tok,
                           shard_toks[2] if (h == 8 and tq == 0) else None)
                        E("vector", lambda e, g=g, hs=hs, qsl=qsl, c=dv.inc():
                          e.tensor_tensor(
                              out=tmpc[:, hs], in0=pD[:, hs],
                              in1=vnT[:, g * 2048:(g + 1) * 2048][:, qsl],
                              op=ALU.mult).then_inc(c.sem, 1))
                        pD_drain[hi] = dv.tok()
                        E("vector", lambda e, h=h, hs=hs, qsl=qsl, c=dv.inc():
                          e.tensor_tensor(
                              out=qyb(h)[:, qsl], in0=ytmp[:, hs],
                              in1=tmpc[:, hs], op=ALU.subtract)
                          .then_inc(c.sem, 1))
                        y_done[h][tq] = dv.tok()
        p3_dve = dv.tok()
        p3_pool = pl.tok()
        p3_pe = pe.tok()
        if "y" in dbg_d:
            ew("scalar", y_done[NHEAD - 1][1])
            dma("scalar", dbg_d["y"], arena_b[:, 0:32768], c_dbg)

        ogrn_free = [None, None]
        oacc_free = [None, None, None, None]
        outT_free = [None] * 16
        x4_free = [None, None]
        x1st = [None]
        pc4_free = None
        pd4_free = None
        xn1_last = None

        for qb in range(4):
            jo_last = None
            for jo in range(16):
                gb = jo % 2
                ew("sync", wT_tile_tok["wo"][jo] if qb == 0 else None,
                   qproj_all if (qb == 0 and jo == 0) else None,
                   ogrn_free[gb])
                g_tok = dma("sync",
                            wogrn[:, gb * 2048:(gb + 1) * 2048].rearrange(
                                "p (jc r) -> p jc r", r=128),
                            wT_r["wo"][:, :, jo * 128:(jo + 1) * 128],
                            c_grn[gb])
                sl = (qb * 16 + jo) % 4
                acc = accslot[sl]
                ew("tensor", g_tok, oacc_free[sl],
                   y_done[NHEAD - 1][qb] if jo == 0 else None)
                for jc in range(16):
                    mm(acc, wogrn[:, gb * 2048 + jc * 128:
                                  gb * 2048 + (jc + 1) * 128],
                       qyb(jc)[:, qb * 512:(qb + 1) * 512],
                       jc == 0, jc == 15, mark=(jc == 15))
                mm_tok = pe.tok()
                ogrn_free[gb] = mm_tok
                ew("scalar", mm_tok,
                   p3_dve if (qb == 0 and jo == 0) else None,
                   p3_pool if (qb == 0 and jo == 0) else None,
                   outT_free[jo])
                E("scalar", lambda e, jo=jo, acc=acc, c=ac.inc(): e.copy(
                    out=outT[jo], in_=acc).then_inc(c.sem, 1))
                oacc_free[sl] = ac.tok()
                jo_last = ac.tok()
            for tb in range(4):
                tglob = qb * 4 + tb
                ew("scalar", x4_free[tb % 2], x1st[0])
                xl_tok = dma("scalar", xtile[tb % 2],
                             x_in[tglob * 128:(tglob + 1) * 128, :],
                             c_x[tb % 2])
                ew("tensor", jo_last, pc4_free, pd4_free,
                   pC_drain[1] if (qb == 0 and tb == 0) else None,
                   pD_drain[1] if (qb == 0 and tb == 0) else None)
                for jo in range(16):
                    dst = pC if jo < 8 else pD
                    off = (jo % 8) * 128
                    tp(dst[:, off:off + 128],
                       outT[jo][:, tb * 128:(tb + 1) * 128], sidf[:],
                       mark=(jo == 15))
                tr_tok = pe.tok()
                if tb == 3:
                    for jo in range(16):
                        outT_free[jo] = tr_tok
                ew("vector", tr_tok, xl_tok, x1st[0])
                E("vector", lambda e, t=xtile[tb % 2]: e.tensor_tensor(
                    out=t[:, 0:1024], in0=t[:, 0:1024], in1=pC[:],
                    op=ALU.add))
                E("vector", lambda e, t=xtile[tb % 2], c=dv.inc():
                  e.tensor_tensor(
                      out=t[:, 1024:2048], in0=t[:, 1024:2048], in1=pD[:],
                      op=ALU.add).then_inc(c.sem, 1))
                x1_tok = dv.tok()
                pc4_free = x1_tok
                pd4_free = x1_tok
                ew("scalar", x1_tok)
                x1st[0] = dma("scalar",
                              x1_d[tglob * 128:(tglob + 1) * 128, :],
                              xtile[tb % 2], c_x1)
                xn_tok, ev_t, smm_t, st_t = ln_chain2(
                    xtile[tb % 2], x1_tok, xn16_p4, tglob,
                    pC[:, 0:128], pD[:, 0:128],
                    p4st[:, 0:128], p4st[:, 128:256], p4st[:, 256:384],
                    tr_wait=(p3_pe if tglob == 0 else None,),
                    ps_wait=(None,))
                x4_free[tb % 2] = xn_tok
                pc4_free = st_t
                pd4_free = st_t
                xn1_last = xn_tok
                oacc_free[0] = smm_t
                oacc_free[1] = smm_t
        xn1T_all = xn1_last
        x1_all = x1st[0]
        p4_pe_done = pe.tok()
        if "x1" in dbg_d:
            ew("scalar", x1_all)
            dma("scalar", dbg_d["x1"], x1_d, stq)

        gag_r = gag_o.ap().rearrange("(r jc p) n -> p r jc n", p=128, jc=16)
        uag_r = uag_o.ap().rearrange("(r jc p) n -> p r jc n", p=128, jc=16)
        dag_r = dag_o.ap().rearrange("(r jn p) h -> p r jn h", p=128, jn=16)
        gsl_free = [None, None]
        usl_free = [None, None]
        dsl_free = [None, None]
        int_free = [None, None]
        gu_free = [None, None]
        dw_free = [None, None]
        acc5_tok = [None] * 16
        scr_free = [None, None]
        x15_free = [None, None]
        out5_free = [None]
        fin_ps_free = None

        for tq4 in range(4):
            for r in range(NCORES):
                it = r % 2
                int_done = None
                for grp in range(8):
                    gb = grp % 2
                    first5 = tq4 == 0 and r == 0 and grp == 0
                    ew("sync",
                       ccw[0].tok() if first5 else None,
                       ccw[1].tok() if first5 else None,
                       p4_pe_done if first5 else None,
                       gsl_free[gb])
                    gt = dma("sync",
                             gslab(gb).rearrange("p (jc r_) -> p jc r_",
                                                 r_=256),
                             gag_r[:, r, :, grp * 256:(grp + 1) * 256],
                             c_gsl[gb])
                    ew("sync", usl_free[gb])
                    ut = dma("sync",
                             uslab(gb).rearrange("p (jc r_) -> p jc r_",
                                                 r_=256),
                             uag_r[:, r, :, grp * 256:(grp + 1) * 256],
                             c_usl[gb])
                    gps, ups = ((pA, pB), (pC, pD))[gb]
                    ew("tensor", gt, ut, gu_free[gb],
                       xn1T_all if first5 else None)
                    for jc in range(16):
                        for k in range(2):
                            mm(gps[:, k * 512:(k + 1) * 512],
                               gslab(gb)[:, jc * 256 + k * 128:
                                         jc * 256 + (k + 1) * 128],
                               xn1T(jc)[:, tq4 * 512:(tq4 + 1) * 512],
                               jc == 0, jc == 15)
                        for k in range(2):
                            mm(ups[:, k * 512:(k + 1) * 512],
                               uslab(gb)[:, jc * 256 + k * 128:
                                         jc * 256 + (k + 1) * 128],
                               xn1T(jc)[:, tq4 * 512:(tq4 + 1) * 512],
                               jc == 0, jc == 15,
                               mark=(jc == 15 and k == 1))
                    gu_tok = pe.tok()
                    gsl_free[gb] = gu_tok
                    usl_free[gb] = gu_tok
                    for k in range(2):
                        jn = grp * 2 + k
                        sc = k
                        A = sq5[:, sc * 512:(sc + 1) * 512]
                        B = st5[:, sc * 512:(sc + 1) * 512]
                        gp = gps[:, k * 512:(k + 1) * 512]
                        up = ups[:, k * 512:(k + 1) * 512]
                        ew("scalar", gu_tok, scr_free[sc])
                        E("scalar", lambda e, gp=gp, A=A, c=ac.inc():
                          e.activation(out=A, in_=gp, func=AF.Square)
                          .then_inc(c.sem, 1))
                        sq_tok = ac.tok()
                        ew("vector", sq_tok,
                           int_free[it] if jn == 0 else None)
                        E("vector", lambda e, A=A, B=B: e.tensor_scalar(
                            out=B, in0=A, scalar1=GC1, scalar2=GC0,
                            op0=ALU.mult, op1=ALU.add))
                        E("vector", lambda e, gp=gp, A=A, B=B, c=dv.inc():
                          e.tensor_tensor(out=A, in0=B, in1=gp, op=ALU.mult)
                          .then_inc(c.sem, 1))
                        arg_tok = dv.tok()
                        ew("scalar", arg_tok)
                        E("scalar", lambda e, A=A, B=B, c=ac.inc():
                          e.activation(out=B, in_=A, func=AF.Tanh)
                          .then_inc(c.sem, 1))
                        th_tok = ac.tok()
                        ew("vector", th_tok)
                        E("vector", lambda e, B=B: e.tensor_scalar(
                            out=B, in0=B, scalar1=0.5, scalar2=0.5,
                            op0=ALU.mult, op1=ALU.add))
                        E("vector", lambda e, gp=gp, A=A, B=B:
                          e.tensor_tensor(out=A, in0=B, in1=gp, op=ALU.mult))
                        E("vector", lambda e, up=up, A=A, it=it, jn=jn,
                          c=dv.inc(): e.tensor_tensor(
                              out=interb(it, jn), in0=A, in1=up, op=ALU.mult)
                          .then_inc(c.sem, 1))
                        scr_free[sc] = dv.tok()
                        if grp == 7 and k == 1:
                            int_done = dv.tok()
                        gu_free[gb] = dv.tok()
                for hg in range(8):
                    db = hg % 2
                    firstd = tq4 == 0 and r == 0 and hg == 0
                    ew("sync",
                       ccw[2].tok() if firstd else None,
                       p3_dve if firstd else None,
                       x1_all if firstd else None,
                       xn1T_all if firstd else None,
                       dsl_free[db])
                    dt = dma("sync",
                             dslab(db).rearrange("p (jn r_) -> p jn r_",
                                                 r_=256),
                             dag_r[:, r, :, hg * 256:(hg + 1) * 256],
                             c_dsl[db])
                    dps = (pA, pB)[db]
                    ew("tensor", dt, int_done if hg == 0 else None,
                       dw_free[db])
                    for jn in range(16):
                        for k in range(2):
                            mm(dps[:, k * 512:(k + 1) * 512],
                               dslab(db)[:, jn * 256 + k * 128:
                                         jn * 256 + (k + 1) * 128],
                               interb(it, jn), jn == 0, jn == 15,
                               mark=(jn == 15 and k == 1))
                    dn_tok = pe.tok()
                    dsl_free[db] = dn_tok
                    if hg == 7:
                        int_free[it] = dn_tok
                    for k in range(2):
                        h = hg * 2 + k
                        ew("vector", dn_tok if k == 0 else None,
                           acc5_tok[h])
                        if r == 0:
                            E("vector", lambda e, h=h, k=k, dps=dps,
                              c=dv.inc(): e.tensor_copy(
                                  out=accF[h],
                                  in_=dps[:, k * 512:(k + 1) * 512])
                              .then_inc(c.sem, 1))
                        else:
                            E("vector", lambda e, h=h, k=k, dps=dps,
                              c=dv.inc(): e.tensor_tensor(
                                  out=accF[h], in0=accF[h],
                                  in1=dps[:, k * 512:(k + 1) * 512],
                                  op=ALU.add).then_inc(c.sem, 1))
                        acc5_tok[h] = dv.tok()
                        dw_free[db] = dv.tok()
                gu_free[0] = dv.tok()
            for tb in range(4):
                tglob = tq4 * 4 + tb
                ew("scalar", x15_free[tb % 2])
                x1l = dma("scalar", xtile[tb % 2],
                          x1_d[tglob * 128:(tglob + 1) * 128, :],
                          c_x[tb % 2])
                ew("tensor", acc5_tok[15], fin_ps_free,
                   gu_free[1] if tb == 0 else None)
                for h in range(16):
                    dst = pC if h < 8 else pD
                    off = (h % 8) * 128
                    tp(dst[:, off:off + 128],
                       accF[h][:, tb * 128:(tb + 1) * 128], sidf[:],
                       mark=(h == 15))
                ftr_tok = pe.tok()
                ew("vector", ftr_tok, x1l, out5_free[0])
                E("vector", lambda e, t=xtile[tb % 2]: e.tensor_tensor(
                    out=t[:, 0:1024], in0=t[:, 0:1024], in1=pC[:],
                    op=ALU.add))
                E("vector", lambda e, t=xtile[tb % 2], c=dv.inc():
                  e.tensor_tensor(
                      out=t[:, 1024:2048], in0=t[:, 1024:2048], in1=pD[:],
                      op=ALU.add).then_inc(c.sem, 1))
                fo_tok = dv.tok()
                fin_ps_free = fo_tok
                ew("scalar", fo_tok)
                out5_free[0] = dma("scalar",
                                   out_d[tglob * 128:(tglob + 1) * 128, :],
                                   xtile[tb % 2], c_dbg)
                x15_free[tb % 2] = out5_free[0]
            if tq4 < 3:
                final_tr = pe.tok()
                for h in range(16):
                    acc5_tok[h] = final_tr
                gu_free[1] = dv.tok()

        final_store = stq.tok()
        E("sync", lambda e, t=final_store: e.wait_ge(t[0], t[1]))

        with nc.Block() as block:
            @block.sync
            def _(eng):
                for g_, f in EV:
                    if g_ == "sync":
                        f(eng)

            @block.scalar
            def _(eng):
                for g_, f in EV:
                    if g_ == "scalar":
                        f(eng)

            @block.vector
            def _(eng):
                for g_, f in EV:
                    if g_ == "vector":
                        f(eng)

            @block.tensor
            def _(eng):
                for g_, f in EV:
                    if g_ == "tensor":
                        f(eng)

            @block.gpsimd
            def _(eng):
                for g_, f in EV:
                    if g_ == "gpsimd":
                        f(eng)

    return nc


_NC_CACHE = {}


def _get_nc(dbg=()):
    key = tuple(x[0] for x in dbg)
    if key not in _NC_CACHE:
        _NC_CACHE[key] = build(dbg)
    return _NC_CACHE[key]


def _consts():
    r4v = np.zeros((4, 128), dtype=np.float32)
    for b_ in range(4):
        r4v[b_, b_ * 32:(b_ + 1) * 32] = 0.25
    return dict(ident16=np.eye(128, dtype=_BF),
                identf=np.eye(128, dtype=np.float32),
                ones16=np.ones((128, 128), dtype=_BF),
                onesf=np.ones((128, 128), dtype=np.float32),
                r4=r4v)


def make_in_maps(x, wq_, wk_, wv_, wo_, gp, gs, up, us, dp, ds):
    consts = _consts()
    in_maps = []
    for c in range(NCORES):
        b, qs = divmod(c, 4)
        rows = slice(c * 2048, (c + 1) * 2048)
        in_maps.append(dict(
            x_in=np.ascontiguousarray(x[b, qs * 2048:(qs + 1) * 2048, :]),
            wq=wq_, wk=wk_, wv=wv_, wo=wo_,
            gpk=np.ascontiguousarray(gp[rows, :]),
            gsct=np.ascontiguousarray(gs[rows, :].T),
            upk=np.ascontiguousarray(up[rows, :]),
            usct=np.ascontiguousarray(us[rows, :].T),
            dpk=np.ascontiguousarray(dp[:, c * 1024:(c + 1) * 1024]),
            dsct=np.ascontiguousarray(ds[:, c * 64:(c + 1) * 64].T),
            **consts))
    return in_maps


def kernel(L0_attn_x, L0_attn_Wq, L0_attn_Wk, L0_attn_Wv, L0_attn_Wo,
           L0_ffn_gate_W_packed, L0_ffn_gate_scales, L0_ffn_up_W_packed,
           L0_ffn_up_scales, L0_ffn_down_W_packed, L0_ffn_down_scales):
    x = np.ascontiguousarray(np.asarray(L0_attn_x, dtype=np.float32))
    in_maps = make_in_maps(
        x,
        np.ascontiguousarray(np.asarray(L0_attn_Wq, dtype=np.float32)),
        np.ascontiguousarray(np.asarray(L0_attn_Wk, dtype=np.float32)),
        np.ascontiguousarray(np.asarray(L0_attn_Wv, dtype=np.float32)),
        np.ascontiguousarray(np.asarray(L0_attn_Wo, dtype=np.float32)),
        np.asarray(L0_ffn_gate_W_packed, dtype=np.int32),
        np.asarray(L0_ffn_gate_scales, dtype=np.float32),
        np.asarray(L0_ffn_up_W_packed, dtype=np.int32),
        np.asarray(L0_ffn_up_scales, dtype=np.float32),
        np.asarray(L0_ffn_down_W_packed, dtype=np.int32),
        np.asarray(L0_ffn_down_scales, dtype=np.float32))
    nc = _get_nc()
    res = run_bass_kernel_spmd(nc, in_maps, core_ids=list(range(NCORES)))
    out = np.empty((2, 8192, D), dtype=np.float32)
    for c in range(NCORES):
        b, qs = divmod(c, 4)
        out[b, qs * 2048:(qs + 1) * 2048, :] = np.asarray(
            res.results[c]["out"], dtype=np.float32)
    return out
